# version 11
# baseline (speedup 1.0000x reference)
import contextlib
import numpy as np
import ml_dtypes
import concourse.bass as bass
import concourse.mybir as mybir
from concourse.bass_utils import run_bass_kernel_spmd

F32 = mybir.dt.float32
BF16 = mybir.dt.bfloat16
AF = mybir.ActivationFunctionType
ALU = mybir.AluOpType
AX = mybir.AxisListType.X

ENGINES = ("sync", "tensor", "vector", "scalar", "gpsimd")

DM = 1024
DH = 1024
L = 4096
LH = 2048
NST = 16
DTR = 64
NES = 128
GRP = 32
NGRP = LH // GRP
LQ = 1024
DEBUG_L2 = False


class Sched:

    paranoid = False

    def __init__(self, nc):
        self.nc = nc
        self.ops = []
        self.pending = {e: [] for e in ENGINES}
        self.last_in_stream = {}

    def op(self, eng, fn, deps=(), hard=(), dma=False):
        deps = [d for d in deps if d is not None] + self.pending[eng]
        if self.paranoid:
            for j in range(len(self.ops) - 1, -1, -1):
                if self.ops[j]["eng"] == eng and self.ops[j]["stream"] is None:
                    deps.append(j)
                    break
        stream = None
        if dma:
            stream = dma if isinstance(dma, str) else "dma"
            prev = self.last_in_stream.get(stream)
            if prev is not None:
                deps.append(prev)
        i = len(self.ops)
        self.ops.append(dict(eng=eng, fn=fn, stream=stream, mark=bool(dma),
                             deps=tuple(deps),
                             hard=tuple(d for d in hard if d is not None)))
        if stream is not None:
            self.last_in_stream[stream] = i
        return i

    def barrier(self):
        last = {}
        for i, o in enumerate(self.ops):
            last[o["eng"] if o["stream"] is None else o["stream"]] = i
        deps = list(last.values())
        for e in ENGINES:
            self.pending[e] = list(deps)

    def build(self):
        nc = self.nc
        for o in self.ops:
            for d in o["deps"]:
                dop = self.ops[d]
                if dop["stream"] is not None or dop["eng"] != o["eng"] or self.paranoid:
                    dop["mark"] = True
            for d in o["hard"]:
                self.ops[d]["mark"] = True
        stk = contextlib.ExitStack()
        keys = list(ENGINES) + sorted({o["stream"] for o in self.ops if o["stream"]})
        sems = {k: stk.enter_context(nc.semaphore("sem_" + k)) for k in keys}
        val = {k: 0 for k in keys}
        for o in self.ops:
            k = o["stream"] or o["eng"]
            if o["mark"]:
                val[k] += 16 if o["stream"] else 1
            o["val"] = val[k]
            o["key"] = k
        final = dict(val)

        with nc.Block() as block:
            for eng_name in ENGINES:
                def mk(e):
                    def body(engine):
                        waited = {}
                        for o in self.ops:
                            if o["eng"] != e:
                                continue
                            for d in tuple(o["deps"]) + tuple(o["hard"]):
                                dop = self.ops[d]
                                tk, tv = dop["key"], dop["val"]
                                if (dop["stream"] is None and dop["eng"] == e
                                        and d not in o["hard"] and not self.paranoid):
                                    continue
                                if waited.get(tk, 0) < tv:
                                    engine.wait_ge(sems[tk], tv)
                                    waited[tk] = tv
                            inst = o["fn"](engine)
                            if o["mark"]:
                                inst.then_inc(sems[o["key"]], 16 if o["stream"] else 1)
                        if e == "sync":
                            for tk in keys:
                                if tk != "sync" and final[tk] > waited.get(tk, 0):
                                    engine.wait_ge(sems[tk], final[tk])
                    return body
                getattr(block, eng_name)(mk(eng_name))
        return stk


def _ln_chunk(s, xb, xout, stat, sqs, dep_load, prev, cvals, cdep, xout_guard=None):
    m2_prev, sqs_prev = prev
    r1 = s.op("vector", lambda e: e.reduce_sum(stat[:, 0:1], xb[:], AX), deps=[dep_load])
    r2 = s.op("scalar", lambda e: e.activation(sqs[:], xb[:], AF.Square, accum_out=stat[:, 1:2]),
              deps=[dep_load, m2_prev, sqs_prev])
    m1 = s.op("vector", lambda e: e.tensor_scalar(stat[:, 2:3], stat[:, 0:1], 1.0 / DM, None, ALU.mult),
              deps=[r2], hard=[r1])
    m2 = s.op("vector", lambda e: e.tensor_scalar(stat[:, 3:4], stat[:, 1:2], 1.0 / DM, None, ALU.mult))
    m3 = s.op("vector", lambda e: e.tensor_tensor(stat[:, 4:5], stat[:, 2:3], stat[:, 2:3], ALU.mult),
              hard=[m1])
    m4 = s.op("vector", lambda e: e.tensor_tensor(stat[:, 5:6], stat[:, 3:4], stat[:, 4:5], ALU.subtract),
              hard=[m2, m3])
    v1 = s.op("scalar", lambda e: e.activation(stat[:, 6:7], stat[:, 5:6], AF.Ln, bias=cvals[:, 0:1]),
              deps=[m4, cdep])
    v2 = s.op("scalar", lambda e: e.activation(stat[:, 7:8], stat[:, 6:7], AF.Exp, scale=cvals[:, 1:2]),
              hard=[v1])
    xn = s.op("vector", lambda e: e.tensor_scalar(
        xout[:], xb[:], stat[:, 2:3], stat[:, 7:8], ALU.subtract, ALU.mult),
        deps=[v2, xout_guard], hard=[m1])
    return xn, r2, m2



def build_L1():
    nc = bass.Bass("TRN2", target_bir_lowering=False)
    x_tm = nc.dram_tensor("x_tm", [L, DM], F32, kind="ExternalInput").ap()
    w_inT = nc.dram_tensor("w_inT", [DM, 2 * DH], F32, kind="ExternalInput").ap()
    convw = nc.dram_tensor("convw", [DH, 4], F32, kind="ExternalInput").ap()
    convb = nc.dram_tensor("convb", [128, 8], F32, kind="ExternalInput").ap()
    xp_T = nc.dram_tensor("xp_T", [DH, 96], F32, kind="ExternalInput").ap()
    identb_d = nc.dram_tensor("identb", [128, 128], BF16, kind="ExternalInput").ap()

    o_xbf = nc.dram_tensor("o_xbf", [DH, L], BF16, kind="ExternalOutput").ap()
    o_szbf = nc.dram_tensor("o_szbf", [DH, L], BF16, kind="ExternalOutput").ap()
    o_xdbl = nc.dram_tensor("o_xdbl", [96, L], F32, kind="ExternalOutput").ap()

    s = Sched(nc)
    sb = nc.alloc_sbuf_tensor
    ps = nc.alloc_psum_tensor

    ident = sb("ident", [128, 128], BF16)
    xn_fm = [sb(f"xn_fm{k}", [128, L], BF16) for k in range(8)]
    w_stage = [sb(f"wst{i}", [128, DH], F32) for i in range(2)]
    w_bf = [sb(f"wbf{k}", [128, 2 * DH], BF16) for k in range(8)]
    cw = sb("cw", [128, 32], F32)
    cb = sb("cb", [128, 8], F32)
    xpt = sb("xpt", [128, 8 * 96], F32)
    xptb = sb("xptb", [128, 8 * 96], BF16)
    xch = [sb(f"xch{i}", [128, DM], F32) for i in range(2)]
    stat = [sb(f"stat{i}", [128, 8], F32) for i in range(2)]
    sqs = [sb(f"sqs{i}", [128, DM], F32) for i in range(2)]
    cvals = sb("cvals", [128, 3], F32)
    xnc = [sb(f"xnc{i}", [128, DM], BF16) for i in range(2)]
    xc = [sb(f"xc{r}", [128, 3 + LQ], F32) for r in range(8)]
    cacc = sb("cacc", [128, LQ], F32)
    cacc2 = [sb(f"cacc2{i}", [128, LQ], F32) for i in range(2)]
    xbf_t = [sb(f"xbf{r}", [128, LQ], BF16) for r in range(8)]
    szbf_t = [sb(f"szbf{i}", [128, LQ], BF16) for i in range(2)]
    xdbl_sb = [sb(f"xdbl_sb{i}", [96, LQ], F32) for i in range(2)]

    tpp = [ps(f"tpp{i}", [128, DM], BF16) for i in range(2)]
    ipp = [ps(f"ipp{i}", [128, 512], F32) for i in range(4)]
    xpp = ps("xpp", [96, LQ], F32)

    d_ident = s.op("sync", lambda e: e.dma_start(out=ident[:], in_=identb_d), dma="const")
    d_cw = s.op("sync", lambda e: e.dma_start(
        out=cw[:].rearrange("p (d k) -> p d k", k=4),
        in_=convw.rearrange("(d p) k -> p d k", p=128)), dma="const")
    d_cb = s.op("sync", lambda e: e.dma_start(out=cb[:], in_=convb), dma="const")
    d_xpt = s.op("sync", lambda e: e.dma_start(
        out=xpt[:].rearrange("p (d n) -> p d n", n=96),
        in_=xp_T.rearrange("(d p) n -> p d n", p=128)), dma="const")
    c_xpt = s.op("vector", lambda e: e.tensor_copy(xptb[:], xpt[:]), deps=[d_xpt])
    cv1 = s.op("vector", lambda e: e.memset(cvals[:, 0:1], 1e-5))
    cv2 = s.op("vector", lambda e: e.memset(cvals[:, 1:2], -0.5))
    cdep = s.op("vector", lambda e: e.memset(cvals[:, 2:3], 1.0))

    wcasts = []
    for kk in range(16):
        half, k = kk // 8, kk % 8
        st = w_stage[kk % 2]
        d = s.op("sync", lambda e, st=st, k=k, half=half: e.dma_start(
            out=st[:], in_=w_inT[128 * k:128 * (k + 1), DH * half:DH * half + DH]),
            deps=[wcasts[kk - 2] if kk >= 2 else None], dma="win%d" % (kk % 2))
        wcasts.append(s.op("vector", lambda e, st=st, k=k, half=half: e.tensor_copy(
            w_bf[k][:, DH * half:DH * half + DH], st[:]), deps=[d]))

    ev_prev = {}
    ln_prev = {}
    reader_prev = {}
    tr_last = {}
    ev_chunk = {}

    def emit_ln(c):
        xb = xch[c % 2]
        d = s.op("sync", lambda e, xb=xb, c=c: e.dma_start(
            out=xb[:], in_=x_tm[128 * c:128 * (c + 1), :]),
            deps=reader_prev.get(c % 2, []), dma="xin%d" % (c % 2))
        xn, r2, m2 = _ln_chunk(s, xb, xnc[c % 2], stat[c % 2], sqs[c % 2], d,
                               ln_prev.get(c % 2, (None, None)), cvals, cdep,
                               xout_guard=tr_last.get(c % 2))
        ln_prev[c % 2] = (m2, None)
        reader_prev[c % 2] = [xn, r2]
        for k in range(8):
            pb = (c * 8 + k) % 2
            t = s.op("tensor", lambda e, pb=pb, k=k, c=c: e.transpose(
                tpp[pb][:, 0:128], xnc[c % 2][:, 128 * k:128 * (k + 1)], ident[:]),
                deps=[xn, d_ident, ev_prev.get(pb)])
            tr_last[c % 2] = t
            ev_prev[pb] = s.op("vector", lambda e, pb=pb, k=k, c=c: e.tensor_copy(
                xn_fm[k][:, 128 * c:128 * (c + 1)], tpp[pb][:, 0:128]), deps=[t])
        ev_chunk[c] = ev_prev[(c * 8 + 7) % 2]

    for c in range(8):
        emit_ln(c)

    czero = [s.op("vector", lambda e, r=r: e.memset(xc[r][:, 0:3], 0.0)) for r in range(8)]

    xproj_prev = None
    psum_guard = {}
    xbf_guard = {}
    szdma_prev = {}
    xc_guard = {}
    sil_par = {}
    sils, dxs = {}, {}
    for lc in range(4):
        xproj_mms = []
        pend_xp = []

        def emit_xproj(dc):
            xms = []
            for nh in range(2):
                xm = s.op("tensor", lambda e, dc=dc, nh=nh: e.matmul(
                    xpp[0:96, 512 * nh:512 * nh + 512],
                    xptb[:, 96 * dc:96 * (dc + 1)],
                    xbf_t[dc][:, 512 * nh:512 * nh + 512],
                    start=(dc == 0), stop=(dc == 7)),
                    deps=[sils[dc], c_xpt] + ([xproj_prev] if (dc == 0 and nh == 0) else []))
                xms.append(xm)
                xproj_mms.append(xm)
            xbf_guard[dc] = [dxs[dc]] + xms
        for rc in range(16):
            if rc >= 8 and lc < 3:
                emit_ln(8 * (lc + 1) + rc - 8)
            pp = ipp[rc % 4]
            evs = []
            for nh in range(2):
                last_mm = None
                dep0 = [psum_guard.get(rc % 4), ev_chunk[8 * lc + 7]]
                if nh == 1:
                    dep0 = [evs[0]]
                for k in range(8):
                    last_mm = s.op("tensor", lambda e, pp=pp, k=k, nh=nh, rc=rc, lc=lc: e.matmul(
                        pp[:],
                        w_bf[k][:, 128 * rc:128 * (rc + 1)],
                        xn_fm[k][:, LQ * lc + 512 * nh:LQ * lc + 512 * nh + 512],
                        start=(k == 0), stop=(k == 7)),
                        deps=[wcasts[8 * (rc // 8) + k]] + (dep0 if k == 0 else []))
                if rc < 8:
                    ev = s.op("scalar", lambda e, pp=pp, rc=rc, nh=nh: e.copy(
                        xc[rc][:, 3 + 512 * nh:3 + 512 * nh + 512], pp[:]),
                        deps=[last_mm, xc_guard.get(rc)])
                else:
                    ev = s.op("scalar", lambda e, pp=pp, rc=rc, nh=nh: e.activation(
                        szbf_t[rc % 2][:, 512 * nh:512 * nh + 512], pp[:], AF.Silu),
                        deps=[last_mm, szdma_prev.get(rc % 2)])
                evs.append(ev)
            psum_guard[rc % 4] = evs[1]
            if rc < 8:
                dc = rc
                a0 = s.op("vector", lambda e, dc=dc: e.tensor_scalar(
                    cacc[:], xc[dc][:, 0:LQ], cw[:, 4 * dc:4 * dc + 1], None, ALU.mult),
                    deps=[evs[1], d_cw, czero[dc]])
                cc2 = cacc2[dc % 2]
                a1 = s.op("vector", lambda e, dc=dc, cc2=cc2: e.scalar_tensor_tensor(
                    cc2[:], xc[dc][:, 1:1 + LQ], cw[:, 4 * dc + 1:4 * dc + 2], cacc[:],
                    ALU.mult, ALU.add), deps=[sil_par.get(dc % 2)])
                a2 = s.op("vector", lambda e, dc=dc, cc2=cc2: e.scalar_tensor_tensor(
                    cacc[:], xc[dc][:, 2:2 + LQ], cw[:, 4 * dc + 2:4 * dc + 3], cc2[:],
                    ALU.mult, ALU.add))
                a3 = s.op("vector", lambda e, dc=dc, cc2=cc2: e.scalar_tensor_tensor(
                    cc2[:], xc[dc][:, 3:3 + LQ], cw[:, 4 * dc + 3:4 * dc + 4], cacc[:],
                    ALU.mult, ALU.add))
                ccp = s.op("vector", lambda e, dc=dc: e.tensor_copy(xc[dc][:, 0:3], xc[dc][:, LQ:LQ + 3]),
                           deps=[a3])
                xc_guard[dc] = ccp
                sil = s.op("scalar", lambda e, dc=dc, cc2=cc2: e.activation(
                    xbf_t[dc][:], cc2[:], AF.Silu, bias=cb[:, dc:dc + 1]),
                    deps=[a3, d_cb] + xbf_guard.get(dc, []))
                sil_par[dc % 2] = sil
                dx = s.op("sync", lambda e, dc=dc, lc=lc: e.dma_start(
                    out=o_xbf[128 * dc:128 * (dc + 1), LQ * lc:LQ * (lc + 1)], in_=xbf_t[dc][:]),
                    deps=[sil], dma="oxbf")
                sils[dc] = sil
                dxs[dc] = dx
                pend_xp.append(dc)
                if len(pend_xp) > 2:
                    emit_xproj(pend_xp.pop(0))
            else:
                zi = rc - 8
                dz = s.op("sync", lambda e, zi=zi, lc=lc: e.dma_start(
                    out=o_szbf[128 * zi:128 * (zi + 1), LQ * lc:LQ * (lc + 1)], in_=szbf_t[zi % 2][:]),
                    deps=[evs[1]], dma="oszbf")
                szdma_prev[zi % 2] = dz
        while pend_xp:
            emit_xproj(pend_xp.pop(0))
        xev = s.op("vector", lambda e, lc=lc: e.tensor_copy(
            xdbl_sb[lc % 2][0:96, :], xpp[0:96, :]),
            deps=[xproj_mms[-1], xproj_prev])
        xproj_prev = s.op("sync", lambda e, lc=lc: e.dma_start(
            out=o_xdbl[:, LQ * lc:LQ * (lc + 1)], in_=xdbl_sb[lc % 2][0:96, :]),
            deps=[xev], dma="oxdbl")
    stk = s.build()
    return nc, stk



def build_L2():
    nc = bass.Bass("TRN2", target_bir_lowering=False)
    xbf_d = nc.dram_tensor("xbf", [DH, L], BF16, kind="ExternalInput").ap()
    szbf_d = nc.dram_tensor("szbf", [DH, L], BF16, kind="ExternalInput").ap()
    xdbl0_d = nc.dram_tensor("xdbl0", [96, L], F32, kind="ExternalInput").ap()
    xdbl1_d = nc.dram_tensor("xdbl1", [96, L], F32, kind="ExternalInput").ap()
    dtpT_d = nc.dram_tensor("dtpT", [DTR, DH], F32, kind="ExternalInput").ap()
    alog_d = nc.dram_tensor("alog", [DH, NST], F32, kind="ExternalInput").ap()
    dvec_d = nc.dram_tensor("dvec", [128, 8], F32, kind="ExternalInput").ap()
    woutT_d = nc.dram_tensor("woutT", [DH, DM], F32, kind="ExternalInput").ap()
    x_own_d = nc.dram_tensor("x_own", [LH, DM], F32, kind="ExternalInput").ap()
    wc_d = nc.dram_tensor("wc", [DM, NES], F32, kind="ExternalInput").ap()
    bc_d = nc.dram_tensor("bc", [NES, 1], F32, kind="ExternalInput").ap()
    tie_d = nc.dram_tensor("tie", [128, LH], F32, kind="ExternalInput").ap()
    w1_d = nc.dram_tensor("w1", [NES, DM, 32], F32, kind="ExternalInput").ap()
    b1p_d = nc.dram_tensor("b1p", [128, 32], F32, kind="ExternalInput").ap()
    w2T_d = nc.dram_tensor("w2T", [NES, 32, DM], F32, kind="ExternalInput").ap()
    b2_d = nc.dram_tensor("b2", [NES, DM], F32, kind="ExternalInput").ap()
    identb_d = nc.dram_tensor("identb", [128, 128], BF16, kind="ExternalInput").ap()

    o_attn = nc.dram_tensor("o_attn", [L, DM], BF16, kind="ExternalOutput").ap()
    o_ffn = nc.dram_tensor("o_ffn", [LH, DM], F32, kind="ExternalOutput").ap()
    bc_dram = nc.dram_tensor("bc_dram", [32, L], BF16, kind="Internal").ap()
    if DEBUG_L2:
        o_mask = nc.dram_tensor("o_mask", [128, LH], BF16, kind="ExternalOutput").ap()
        o_logit = nc.dram_tensor("o_logit", [128, LH], F32, kind="ExternalOutput").ap()
        o_xot = nc.dram_tensor("o_xot", [LH, DM], BF16, kind="ExternalOutput").ap()
        o_perm = nc.dram_tensor("o_perm", [128, 8 * 64 * 32], BF16, kind="ExternalOutput").ap()

    s = Sched(nc)
    sb = nc.alloc_sbuf_tensor

    ident = sb("ident", [128, 128], BF16)
    xdbl = sb("xdbl", [96, L], F32)
    xot = [sb(f"xot{c}", [128, DM], BF16) for c in range(16)]
    mask_all = sb("mask_all", [128, LH], BF16)
    maskT = sb("maskT", [128, (NGRP // 4) * 128], BF16)
    amat = sb("amat", [128, 8 * NST], F32)
    dvec = sb("dvec_sb", [128, 8], F32)
    dtpb = sb("dtpb", [DTR, DH], F32)
    carry = [sb(f"carry{r}", [128, NST], BF16) for r in range(8)]
    b2b = sb("b2b", [128, DM], BF16)
    b1pk = sb("b1pk", [128, 32], F32)
    cvals = sb("cvals", [128, 3], F32)

    d_ident = s.op("sync", lambda e: e.dma_start(out=ident[:], in_=identb_d), dma="const")
    cv1 = s.op("vector", lambda e: e.memset(cvals[:, 0:1], 1e-5))
    cv2 = s.op("vector", lambda e: e.memset(cvals[:, 1:2], -0.5))
    cdep = s.op("vector", lambda e: e.memset(cvals[:, 2:3], 1.0))
    da = s.op("sync", lambda e: e.dma_start(
        out=amat[:].rearrange("p (d n) -> p d n", n=NST),
        in_=alog_d.rearrange("(d p) n -> p d n", p=128)), dma="const")
    ae = s.op("scalar", lambda e: e.activation(amat[:], amat[:], AF.Exp), deps=[da])
    an = s.op("vector", lambda e: e.tensor_scalar(amat[:], amat[:], -1.0, None, ALU.mult), deps=[ae])
    dd = s.op("sync", lambda e: e.dma_start(out=dvec[:], in_=dvec_d), dma="const")
    ddtp = s.op("sync", lambda e: e.dma_start(out=dtpb[0:DTR, :], in_=dtpT_d), dma="const")
    db1 = s.op("sync", lambda e: e.dma_start(out=b1pk[:], in_=b1p_d), dma="const")

    A = contextlib.ExitStack()
    sbA = lambda *a: A.enter_context(nc.sbuf_tensor(*a))
    psA = lambda *a: A.enter_context(nc.psum_tensor(*a))
    xdbl_p1 = sbA("xdbl_p1", [96, L], F32)
    stat = [sbA(f"stat{i}", [128, 8], F32) for i in range(2)]
    sqs = [sbA(f"sqs{i}", [128, DM], F32) for i in range(2)]
    xch = [sbA(f"xch{i}", [128, DM], F32) for i in range(2)]
    b2stg = sbA("b2stg", [128, DM], F32)
    xdbl_bc = sbA("xdbl_bc", [32, L], BF16)
    xof_hi = [sbA(f"xofh{k}", [128, LH], BF16) for k in range(8)]
    xof_lo = [sbA(f"xofl{k}", [128, LH], BF16) for k in range(8)]
    logit = sbA("logit", [128, LH], F32)
    mxa = sbA("mxa", [128, NGRP], F32)
    ties = sbA("ties", [128, LH], F32)
    wcs = sbA("wcs", [128, 8 * NES], F32)
    wch = sbA("wch", [128, 8 * NES], BF16)
    wcl = sbA("wcl", [128, 8 * NES], BF16)
    wclf = sbA("wclf", [128, 8 * NES], F32)
    bcs = sbA("bcs", [128, 1], F32)
    xlo_c = [sbA(f"xlo{i}", [128, DM], BF16) for i in range(2)]
    tppA = [psA(f"tppA{i}", [128, DM], BF16) for i in range(2)]
    lgp = [psA(f"lgp{i}", [128, 512], F32) for i in range(2)]
    mtp = psA("mtp", [128, 128], BF16)

    db2 = s.op("sync", lambda e: e.dma_start(out=b2stg[:], in_=b2_d), dma="const")
    cb2 = s.op("vector", lambda e: e.tensor_copy(b2b[:], b2stg[:]), deps=[db2])
    d0 = s.op("sync", lambda e: e.dma_start(out=xdbl[0:96, :], in_=xdbl0_d), dma="xdl")
    d1 = s.op("sync", lambda e: e.dma_start(out=xdbl_p1[0:96, :], in_=xdbl1_d), dma="xdl")
    xsum = s.op("vector", lambda e: e.tensor_add(xdbl[0:96, :], xdbl[0:96, :], xdbl_p1[0:96, :]),
                deps=[d0, d1])
    xcast = s.op("vector", lambda e: e.tensor_copy(xdbl_bc[0:32, :], xdbl[64:96, :]), deps=[xsum])
    dbc = s.op("sync", lambda e: e.dma_start(out=bc_dram, in_=xdbl_bc[0:32, :]), deps=[xcast], dma="xdl")

    dwc = s.op("sync", lambda e: e.dma_start(
        out=wcs[:].rearrange("p (d n) -> p d n", n=NES),
        in_=wc_d.rearrange("(d p) n -> p d n", p=128)), dma="const")
    dbcb = s.op("sync", lambda e: e.dma_start(out=bcs[:], in_=bc_d), dma="const")
    dtie = s.op("sync", lambda e: e.dma_start(out=ties[:], in_=tie_d), dma="const")
    ch1 = s.op("vector", lambda e: e.tensor_copy(wch[:], wcs[:]), deps=[dwc])
    ch2 = s.op("vector", lambda e: e.tensor_sub(wclf[:], wcs[:], wch[:]))
    ch3 = s.op("vector", lambda e: e.tensor_copy(wcl[:], wclf[:]))

    ev_prev = {}
    ln_prev = {}
    reader_prev = {}
    t2_last = {}
    for c in range(16):
        xb = xch[c % 2]
        d = s.op("sync", lambda e, xb=xb, c=c: e.dma_start(
            out=xb[:], in_=x_own_d[128 * c:128 * (c + 1), :]),
            deps=reader_prev.get(c % 2, []), dma="xin")
        st_, sq_ = stat[c % 2], sqs[c % 2]
        xn, r2, m2 = _ln_chunk(s, xb, xot[c], st_, sq_, d, ln_prev.get(c % 2, (None, None)),
                               cvals, cdep)
        lo1 = s.op("vector", lambda e, xb=xb, st_=st_, sq_=sq_: e.tensor_scalar(
            sq_[:], xb[:], st_[:, 2:3], st_[:, 7:8], ALU.subtract, ALU.mult), hard=[xn], deps=[r2])
        lo2 = s.op("vector", lambda e, c=c, sq_=sq_: e.tensor_sub(xlo_c[c % 2][:], sq_[:], xot[c][:]),
                   deps=[t2_last.get(c % 2)])
        ln_prev[c % 2] = (m2, lo2)
        reader_prev[c % 2] = [lo2, r2]
        for k in range(8):
            pb = (c * 8 + k) % 2
            t1 = s.op("tensor", lambda e, pb=pb, k=k, c=c: e.transpose(
                tppA[pb][:, 0:128], xot[c][:, 128 * k:128 * (k + 1)], ident[:]),
                deps=[xn, d_ident, ev_prev.get(pb)])
            e1 = s.op("vector", lambda e, pb=pb, k=k, c=c: e.tensor_copy(
                xof_hi[k][:, 128 * c:128 * (c + 1)], tppA[pb][:, 0:128]), deps=[t1])
            t2 = s.op("tensor", lambda e, pb=pb, k=k, c=c: e.transpose(
                tppA[pb][:, 128:256], xlo_c[c % 2][:, 128 * k:128 * (k + 1)], ident[:]),
                deps=[lo2, e1])
            t2_last[c % 2] = t2
            ev_prev[pb] = s.op("vector", lambda e, pb=pb, k=k, c=c: e.tensor_copy(
                xof_lo[k][:, 128 * c:128 * (c + 1)], tppA[pb][:, 128:256]), deps=[t2])
    ln_done = list(ev_prev.values())

    lg_ev = []
    for q in range(LH // 512):
        pp = lgp[q % 2]
        mm = None
        for p3 in range(3):
            lhs, rhs = [(wch, xof_hi), (wch, xof_lo), (wcl, xof_hi)][p3]
            for k in range(8):
                mm = s.op("tensor", lambda e, pp=pp, k=k, q=q, lhs=lhs, rhs=rhs, p3=p3: e.matmul(
                    pp[:], lhs[:, NES * k:NES * (k + 1)], rhs[k][:, 512 * q:512 * q + 512],
                    start=(p3 == 0 and k == 0), stop=(p3 == 2 and k == 7)),
                    deps=(([ch1, ch3] + ln_done + [lg_ev[q - 2] if q >= 2 else None])
                          if (p3 == 0 and k == 0) else []))
        lg_ev.append(s.op("scalar", lambda e, pp=pp, q=q: e.activation(
            logit[:, 512 * q:512 * q + 512], pp[:], AF.Identity, bias=bcs[:]), deps=[mm, dbcb]))
    tie_add = s.op("vector", lambda e: e.tensor_add(logit[:], logit[:], ties[:]), deps=lg_ev + [dtie])
    red = s.op("vector", lambda e: e.tensor_reduce(
        mxa[:], logit[:].rearrange("p (g t) -> p g t", t=GRP), AX, ALU.max), deps=[tie_add])
    mk_ops = []
    for g in range(NGRP):
        mk_ops.append(s.op("vector", lambda e, g=g: e.tensor_scalar(
            mask_all[:, GRP * g:GRP * (g + 1)], logit[:, GRP * g:GRP * (g + 1)],
            mxa[:, g:g + 1], None, ALU.is_equal), hard=[red]))
    if DEBUG_L2:
        s.op("sync", lambda e: e.dma_start(out=o_logit, in_=logit[:]), deps=[red], dma="dbg")
    mt_prev = None
    for g in range(NGRP):
        t = s.op("tensor", lambda e, g=g: e.transpose(
            mtp[32 * (g % 4):32 * (g % 4) + 32, 0:128], mask_all[:, GRP * g:GRP * (g + 1)], ident[:],
            tile_position=(0, 32 * (g % 4))),
            deps=[mk_ops[g], mt_prev])
        mt_prev = s.op("vector", lambda e, g=g: e.tensor_copy(
            maskT[32 * (g % 4):32 * (g % 4) + 32, 128 * (g // 4):128 * (g // 4) + 128],
            mtp[32 * (g % 4):32 * (g % 4) + 32, 0:128]), deps=[t])

    if DEBUG_L2:
        s.op("sync", lambda e: e.dma_start(out=o_mask, in_=mask_all[:]), deps=mk_ops, dma="dbg")
        for c in range(16):
            s.op("sync", lambda e, c=c: e.dma_start(
                out=o_xot[128 * c:128 * (c + 1), :], in_=xot[c][:]), deps=[mt_prev], dma="dbg")
    A.close()
    s.barrier()

    B = contextlib.ExitStack()
    sbB = lambda *a: B.enter_context(nc.sbuf_tensor(*a))
    psB = lambda *a: B.enter_context(nc.psum_tensor(*a))
    bbc = sbB("bbc", [128, NST * LQ], BF16)
    cbc = sbB("cbc", [128, NST * LQ], BF16)
    h_rot = [sbB(f"hrot{i}", [128, LQ], BF16) for i in range(2)]
    out_bf = [[sbB(f"obf{p}_{r}", [128, LQ], BF16) for r in range(8)] for p in range(2)]
    woT = [sbB(f"wot{k}", [128, DM], BF16) for k in range(8)]
    wost = [sbB("wost0", [128, 256], F32)] * 2
    xbf_t = [sbB(f"xbft{i}", [128, LQ], BF16) for i in range(2)]
    szb_t = [sbB(f"szbt{i}", [128, LQ], BF16) for i in range(2)]
    delt = [sbB(f"delt{i}", [128, LQ], BF16) for i in range(2)]
    delx = [sbB(f"delx{i}", [128, LQ], BF16) for i in range(2)]
    ebuf = [sbB(f"ebuf{i}", [128, LQ], BF16) for i in range(2)]
    wbuf = sbB("wbuf", [128, LQ], BF16)
    pbuf = [sbB(f"pbuf{i}", [128, LQ], BF16) for i in range(2)]
    at_sb = [sbB("atsb0", [128, 512], BF16)] * 2
    dt_ps = psB("dt_ps", [128, LQ], F32)
    y_ps = psB("y_ps", [128, LQ], F32)
    at_ps = [psB(f"at_ps{i}", [128, 512], F32) for i in range(2)]

    wo_casts = []
    for kk in range(32):
        k, half = kk // 4, kk % 4
        st = wost[kk % 2]
        d = s.op("sync", lambda e, st=st, k=k, half=half: e.dma_start(
            out=st[:], in_=woutT_d[128 * k:128 * (k + 1), 256 * half:256 * half + 256]),
            deps=[wo_casts[kk - 1] if kk >= 1 else None], dma="wo")
        wo_casts.append(s.op("vector", lambda e, st=st, k=k, half=half: e.tensor_copy(
            woT[k][:, 256 * half:256 * half + 256], st[:]), deps=[d]))
    zinit = [s.op("vector", lambda e, r=r: e.memset(carry[r][:], 0.0)) for r in range(8)]

    bn_readers, cn_readers = {}, {}
    x_readers = {}
    chains = {}

    def emit_chain(g):
        lq, dc = divmod(g, 8)
        dci = g % 2
        dxt = s.op("sync", lambda e, dc=dc, lq=lq, dci=dci: e.dma_start(
            out=xbf_t[dci][:], in_=xbf_d[128 * dc:128 * (dc + 1), LQ * lq:LQ * (lq + 1)]),
            deps=x_readers.get(dci, []), dma="xl")
        dst = s.op("sync", lambda e, dc=dc, lq=lq, dci=dci: e.dma_start(
            out=szb_t[dci][:], in_=szbf_d[128 * dc:128 * (dc + 1), LQ * lq:LQ * (lq + 1)]),
            deps=x_readers.get(dci, []), dma="szl")
        dtm = None
        for nh in range(2):
            dtm = s.op("tensor", lambda e, dc=dc, lq=lq, nh=nh: e.matmul(
                dt_ps[:, 512 * nh:512 * nh + 512],
                dtpb[0:DTR, 128 * dc:128 * (dc + 1)],
                xdbl[0:DTR, LQ * lq + 512 * nh:LQ * lq + 512 * nh + 512],
                start=True, stop=True),
                deps=[xsum, ddtp, state["dtexp"]] if nh == 0 else [])
        ex = s.op("scalar", lambda e, dci=dci: e.activation(delx[dci][:], dt_ps[:], AF.Exp),
                  deps=[dtm] + x_readers.get(dci, []))
        state["dtexp"] = ex
        splus = s.op("scalar", lambda e, dci=dci: e.activation(
            delt[dci][:], delx[dci][:], AF.Ln, bias=cvals[:, 2:3]),
            deps=[ex, cdep] + delt_guard.get(dci, []))
        dxm = s.op("vector", lambda e, dci=dci: e.tensor_mul(delx[dci][:], delt[dci][:], xbf_t[dci][:]),
                   deps=[splus, dxt])
        return (dxt, dst, splus, dxm)

    state = {"dtexp": None}
    gate_prev = None
    pending_opj = []
    outproj_guard = {}
    opj_gen = [None]

    def _opj_ops(lq, g2dep):
        for tc in range(8):
            pb = tc % 2
            for nh in range(2):
                mm = None
                for dc in range(8):
                    mm = s.op("tensor", lambda e, dc=dc, nh=nh, tc=tc, pb=pb, lq=lq: e.matmul(
                        at_ps[pb][:],
                        out_bf[lq % 2][dc][:, 128 * tc:128 * (tc + 1)],
                        woT[dc][:, 512 * nh:512 * nh + 512],
                        start=(dc == 0), stop=(dc == 7)),
                        deps=([g2dep, wo_casts[31], atev_prev.get(pb)] if dc == 0 else []))
                ev = s.op("vector", lambda e, pb=pb, nh=nh: e.tensor_copy(
                    at_sb[pb][:], at_ps[pb][:]),
                    deps=[mm, atdma_prev.get("last")])
                atev_prev[pb] = ev
                for dc in range(8):
                    outproj_guard[(lq % 2, dc)] = mm
                atdma_prev["last"] = s.op("sync", lambda e, pb=pb, tc=tc, lq=lq, nh=nh: e.dma_start(
                    out=o_attn[LQ * lq + 128 * tc:LQ * lq + 128 * (tc + 1), 512 * nh:512 * nh + 512],
                    in_=at_sb[pb][:]),
                    deps=[atev_prev[pb]], dma="oattn")
            yield

    def pump_outproj(lq, g2dep, all_now=False):
        if opj_gen[0] is None:
            opj_gen[0] = _opj_ops(lq, g2dep)
        try:
            while True:
                next(opj_gen[0])
                if not all_now:
                    return
        except StopIteration:
            opj_gen[0] = None

    ebuf_guard = {}
    hrot_guard = {}
    pbuf_guard = {}
    delt_guard = {}
    carry_guard = {}
    outproj_last = None
    atev_prev = {}
    atdma_prev = {}
    it = 0
    for lq in range(4):
        bld_b, bld_c = {}, {}
        for n in range(NST):
            bld_b[n] = s.op("sync", lambda e, n=n, lq=lq: e.dma_start(
                out=bbc[:, LQ * n:LQ * (n + 1)],
                in_=bc_dram[n:n + 1, LQ * lq:LQ * (lq + 1)].broadcast_to([128, LQ])),
                deps=[dbc, bn_readers.get(n)], dma="bl%d" % (n % 4))
            bld_c[n] = s.op("sync", lambda e, n=n, lq=lq: e.dma_start(
                out=cbc[:, LQ * n:LQ * (n + 1)],
                in_=bc_dram[NST + n:NST + n + 1, LQ * lq:LQ * (lq + 1)].broadcast_to([128, LQ])),
                deps=[dbc, cn_readers.get(n)], dma="cl%d" % (n % 4))
        for dc in range(8):
            g = lq * 8 + dc
            dci = g % 2
            if g == 0:
                chains[0] = emit_chain(0)
            ch = chains.pop(g)
            dxt, dst, splus, dxm = ch
            pm = None
            for n in range(NST):
                ib = it % 2
                eo = s.op("scalar", lambda e, ib=ib, dci=dci, dc=dc, n=n: e.activation(
                    ebuf[ib][:], delt[dci][:], AF.Exp, scale=amat[:, NST * dc + n:NST * dc + n + 1]),
                    deps=[splus, an, ebuf_guard.get(ib)])
                wo = s.op("vector", lambda e, dci=dci, n=n: e.tensor_mul(
                    wbuf[:], delx[dci][:], bbc[:, LQ * n:LQ * (n + 1)]),
                    deps=[dxm, bld_b[n]])
                sc = s.op("vector", lambda e, ib=ib, n=n, dc=dc: e.tensor_tensor_scan(
                    h_rot[ib][:], ebuf[ib][:], wbuf[:], carry[dc][:, n:n + 1], ALU.mult, ALU.add),
                    deps=[eo, wo, hrot_guard.get(ib), carry_guard.get((dc, n))], hard=[zinit[dc]])
                ebuf_guard[ib] = sc
                cu = s.op("gpsimd", lambda e, dc=dc, n=n, ib=ib: e.tensor_copy(
                    carry[dc][:, n:n + 1], h_rot[ib][:, LQ - 1:LQ]), deps=[sc])
                carry_guard[(dc, n)] = cu
                po = s.op("vector", lambda e, n=n, ib=ib: e.tensor_mul(
                    pbuf[ib][:], h_rot[ib][:], cbc[:, LQ * n:LQ * (n + 1)]),
                    deps=[sc, bld_c[n], pbuf_guard.get(ib)])
                if dc == 7:
                    bn_readers[n] = wo
                    cn_readers[n] = po
                hrot_guard[ib] = cu
                for nh in range(2):
                    pm = s.op("tensor", lambda e, n=n, nh=nh, ib=ib: e.matmul(
                        y_ps[:, 512 * nh:512 * nh + 512], ident[:],
                        pbuf[ib][:, 512 * nh:512 * nh + 512],
                        start=(n == 0), stop=(n == NST - 1)),
                        deps=[po, d_ident] + ([gate_prev] if (n == 0 and nh == 0) else []))
                pbuf_guard[ib] = pm
                it += 1
                if n == 1 and g + 1 < 32:
                    chains[g + 1] = emit_chain(g + 1)
            g1 = s.op("vector", lambda e, dci=dci, dc=dc: e.scalar_tensor_tensor(
                wbuf[:], xbf_t[dci][:], dvec[:, dc:dc + 1], y_ps[:], ALU.mult, ALU.add),
                deps=[pm, dd])
            g2 = s.op("vector", lambda e, dci=dci, dc=dc, lq=lq: e.tensor_mul(
                out_bf[lq % 2][dc][:], wbuf[:], szb_t[dci][:]),
                deps=[g1, dst, outproj_guard.get((lq % 2, dc))])
            gate_prev = g1
            delt_guard[dci] = [g1]
            x_readers[dci] = [dxm, g1, g2]
            for _ in range(2):
                if pending_opj and pending_opj[0][0] < lq:
                    pump_outproj(*pending_opj[0])
                    if opj_gen[0] is None:
                        pending_opj.pop(0)
        pending_opj.append((lq, g2))
        if lq == 3:
            while pending_opj:
                pump_outproj(*pending_opj.pop(0), all_now=True)
    B.close()
    s.barrier()

    NG2 = NGRP // 2
    C = contextlib.ExitStack()
    sbC = lambda *a: C.enter_context(nc.sbuf_tensor(*a))
    psC = lambda *a: C.enter_context(nc.psum_tensor(*a))
    perm = sbC("perm", [128, 8 * 64 * NG2], BF16)
    eo = [sbC(f"eo{hh}", [128, 8 * NG2 * 64], BF16) for hh in range(2)]
    w1st = [sbC(f"w1st{i}", [128, 256], F32) for i in range(4)]
    w1b = [sbC(f"w1b{i}", [128, 256], BF16) for i in range(4)]
    w2st = [sbC(f"w2st{i}", [128, DM], F32) for i in range(2)]
    w2b = [sbC(f"w2b{i}", [128, DM], BF16) for i in range(4)]
    inner = sbC("inner", [128, NG2 * 16], BF16)
    eot = [sbC(f"eot{i}", [128, DM], BF16) for i in range(2)]
    ffn_sb = [sbC(f"ffnsb{i}", [128, DM], F32) for i in range(2)]
    pg_ps = [psC(f"pg_ps{i}", [128, 512], F32) for i in range(2)]
    in_ps = [psC("in_ps0", [128, NG2], F32)] * 2
    eo_ps = [psC(f"eo_ps{i}", [128, 8 * NG2], F32) for i in range(2)]
    eot_ps = psC("eot_ps", [128, DM], BF16)
    res_ps = psC("res_ps", [128, DM], F32)

    w1_last = {}
    w2_last = {}
    w2c_last = {}
    eot_guard = None
    res_evac = None
    ffn_dma = {}
    perm_guard = []
    eo_guard = []
    for gb in range(2):
        eo_done = []
        for hh in range(2):
            gev = {}
            for gg in range(NG2):
                g = gb * NG2 + gg
                pb = gg % 2
                mm = None
                for dmc in range(8):
                    mm = s.op("tensor", lambda e, g=g, dmc=dmc, pb=pb, hh=hh: e.matmul(
                        pg_ps[pb][:, 64 * dmc:64 * (dmc + 1)],
                        xot[g // 4][32 * (g % 4):32 * (g % 4) + 32, 128 * dmc:128 * (dmc + 1)],
                        maskT[32 * (g % 4):32 * (g % 4) + 32,
                              128 * (g // 4) + 64 * hh:128 * (g // 4) + 64 * hh + 64],
                        start=True, stop=True, tile_position=(32 * (g % 4), 0)),
                        deps=([gev.get(pb)] + perm_guard) if dmc == 0 else [])
                gev[pb] = s.op("scalar", lambda e, gg=gg, pb=pb: e.copy(
                    perm[:].rearrange("p (d n g_) -> p d n g_", n=64, g_=NG2)[:, :, :, gg],
                    pg_ps[pb][:].rearrange("p (d n) -> p d n", n=64)), deps=[mm])
            perm_guard = []
            gather_done = list(gev.values())
            if DEBUG_L2 and gb == 0 and hh == 0:
                gather_done.append(s.op("sync", lambda e: e.dma_start(out=o_perm, in_=perm[:]),
                                        deps=gather_done, dma="dbg"))
            relu_ops = []
            eoev = {}
            for j in range(16):
                jg = 16 * hh + j
                mm_stack = []
                for i in range(4):
                    esp = 4 * j + i
                    es = 64 * hh + esp
                    st, bf = w1st[esp % 4], w1b[esp % 4]
                    dw = s.op("sync", lambda e, es=es, st=st: e.dma_start(
                        out=st[:].rearrange("p (d f) -> p d f", f=32),
                        in_=w1_d[es:es + 1].rearrange("o (d p) f -> p (o d) f", p=128)),
                        deps=[w1_last.get(esp % 4)], dma="w1s%d" % (esp % 4))
                    cw_ = s.op("vector", lambda e, st=st, bf=bf: e.tensor_copy(bf[:], st[:]), deps=[dw])
                    for dmc in range(8):
                        mm = s.op("tensor", lambda e, esp=esp, i=i, dmc=dmc, bf=bf, j=j: e.matmul(
                            in_ps[j % 2][32 * i:32 * i + 32, :],
                            bf[:, 32 * dmc:32 * (dmc + 1)],
                            perm[:, 2048 * dmc + 32 * esp:2048 * dmc + 32 * esp + NG2],
                            start=(dmc == 0), stop=(dmc == 7), tile_position=(0, 32 * i)),
                            deps=[cw_] + (gather_done if esp == 0 else [])
                                 + ([relu_ops[j - 1]] if (dmc == 0 and i == 0 and j >= 1) else []))
                        mm_stack.append(mm)
                        w1_last[esp % 4] = mm
                relu_ops.append(s.op("scalar", lambda e, j=j, jg=jg: e.activation(
                    inner[:, NG2 * j:NG2 * (j + 1)], in_ps[j % 2][:], AF.Relu, bias=b1pk[:, jg:jg + 1]),
                    deps=mm_stack + [db1]))
                st2, bf2 = w2st[j % 2], w2b[j % 4]
                dw2 = s.op("sync", lambda e, jg=jg, st=st2: e.dma_start(
                    out=st[:],
                    in_=w2T_d[4 * jg:4 * jg + 4].rearrange("e f m -> (e f) m")),
                    deps=[w2_last.get(j % 4), w2c_last.get(j % 2)], dma="w2s%d" % (j % 2))
                cw2 = s.op("vector", lambda e, st=st2, bf=bf2: e.tensor_copy(bf[:], st[:]), deps=[dw2])
                w2c_last[j % 2] = cw2
                for i in range(4):
                    esp = 4 * j + i
                    pb = esp % 2
                    mm = None
                    for dmc in range(8):
                        mm = s.op("tensor", lambda e, i=i, dmc=dmc, bf=bf2, j=j, pb=pb: e.matmul(
                            eo_ps[pb][:, NG2 * dmc:NG2 * (dmc + 1)],
                            bf[32 * i:32 * i + 32, 128 * dmc:128 * (dmc + 1)],
                            inner[32 * i:32 * i + 32, NG2 * j:NG2 * (j + 1)],
                            start=True, stop=True, tile_position=(32 * i, 0)),
                            deps=([cw2, relu_ops[j], eoev.get(pb)] + eo_guard) if dmc == 0 else [])
                        w2_last[j % 4] = mm
                    eoev[pb] = s.op("vector", lambda e, esp=esp, pb=pb, hh=hh: e.tensor_copy(
                        eo[hh][:].rearrange("p (d g_ n) -> p d g_ n", g_=NG2, n=64)[:, :, :, esp],
                        eo_ps[pb][:].rearrange("p (d g_) -> p d g_", g_=NG2)),
                        deps=[mm])
                eo_guard = []
            eo_done += list(eoev.values())
            perm_guard += [w1_last[k] for k in w1_last]
        for gg in range(NG2):
            g = gb * NG2 + gg
            ib = gg % 2
            trs = []
            for hh in range(2):
                for dmc in range(8):
                    trs.append(s.op("tensor", lambda e, gg=gg, hh=hh, dmc=dmc: e.transpose(
                        eot_ps[64 * hh:64 * hh + 64, 128 * dmc:128 * (dmc + 1)],
                        eo[hh][:, 2048 * dmc + 64 * gg:2048 * dmc + 64 * gg + 64], ident[:],
                        tile_position=(0, 64 * hh)),
                        deps=(eo_done + [eot_guard]) if (hh == 0 and dmc == 0 and gg < 2) else
                             ([eot_guard] if (hh == 0 and dmc == 0) else [])))
            tev = s.op("scalar", lambda e, ib=ib: e.copy(eot[ib][:], eot_ps[:]), deps=[trs[-1]])
            eot_guard = tev
            mm = None
            first = (g % 4 == 0)
            for nh in range(2):
                mm = s.op("tensor", lambda e, g=g, nh=nh, ib=ib: e.matmul(
                    res_ps[32 * (g % 4):32 * (g % 4) + 32, 512 * nh:512 * nh + 512],
                    mask_all[:, GRP * g:GRP * (g + 1)],
                    eot[ib][:, 512 * nh:512 * nh + 512],
                    start=True, stop=False, tile_position=(0, 32 * (g % 4))),
                    deps=[tev] + ([res_evac] if (first and nh == 0) else []))
                mm = s.op("tensor", lambda e, g=g, nh=nh: e.matmul(
                    res_ps[32 * (g % 4):32 * (g % 4) + 32, 512 * nh:512 * nh + 512],
                    mask_all[:, GRP * g:GRP * (g + 1)],
                    b2b[:, 512 * nh:512 * nh + 512],
                    start=False, stop=True, tile_position=(0, 32 * (g % 4))),
                    deps=[cb2] if (g < 2 and nh == 0) else [])
            if g % 4 == 3:
                sg = g // 4
                rev = s.op("vector", lambda e, sg=sg: e.tensor_copy(ffn_sb[sg % 2][:], res_ps[:]),
                           deps=[mm, ffn_dma.get(sg % 2)])
                res_evac = rev
                ffn_dma[sg % 2] = s.op("sync", lambda e, sg=sg: e.dma_start(
                    out=o_ffn[128 * sg:128 * (sg + 1), :], in_=ffn_sb[sg % 2][:]), deps=[rev], dma="offn")
        eo_guard = [trs[-1]]
    C.close()
    stk = s.build()
    return nc, stk



def build_L3():
    nc = bass.Bass("TRN2", target_bir_lowering=False)
    x_own = nc.dram_tensor("x_own", [LH, DM], F32, kind="ExternalInput").ap()
    attn_a = nc.dram_tensor("attn_a", [LH, DM], BF16, kind="ExternalInput").ap()
    attn_b = nc.dram_tensor("attn_b", [LH, DM], BF16, kind="ExternalInput").ap()
    ffn = nc.dram_tensor("ffn", [LH, DM], F32, kind="ExternalInput").ap()
    o = nc.dram_tensor("o", [LH, DM], F32, kind="ExternalOutput").ap()

    s = Sched(nc)
    sb = nc.alloc_sbuf_tensor
    ta = [sb(f"ta{i}", [128, DM], F32) for i in range(2)]
    tb = [sb(f"tb{i}", [128, DM], BF16) for i in range(2)]
    tc_ = [sb(f"tc{i}", [128, DM], BF16) for i in range(2)]
    td = [sb(f"td{i}", [128, DM], F32) for i in range(2)]
    guard = {}
    for c in range(LH // 128):
        i = c % 2
        d1 = s.op("sync", lambda e, i=i, c=c: e.dma_start(
            out=ta[i][:], in_=x_own[128 * c:128 * (c + 1), :]), deps=guard.get(i, []), dma="la")
        d2 = s.op("sync", lambda e, i=i, c=c: e.dma_start(
            out=tb[i][:], in_=attn_a[128 * c:128 * (c + 1), :]), deps=guard.get(i, []), dma="lb")
        d3 = s.op("sync", lambda e, i=i, c=c: e.dma_start(
            out=tc_[i][:], in_=attn_b[128 * c:128 * (c + 1), :]), deps=guard.get(i, []), dma="lc")
        d4 = s.op("sync", lambda e, i=i, c=c: e.dma_start(
            out=td[i][:], in_=ffn[128 * c:128 * (c + 1), :]), deps=guard.get(i, []), dma="ld")
        a1 = s.op("vector", lambda e, i=i: e.tensor_add(ta[i][:], ta[i][:], tb[i][:]), deps=[d1, d2])
        a2 = s.op("vector", lambda e, i=i: e.tensor_add(ta[i][:], ta[i][:], tc_[i][:]), deps=[d3])
        a3 = s.op("vector", lambda e, i=i: e.tensor_add(ta[i][:], ta[i][:], td[i][:]), deps=[d4])
        dd = s.op("sync", lambda e, i=i, c=c: e.dma_start(
            out=o[128 * c:128 * (c + 1), :], in_=ta[i][:]), deps=[a3], dma="out")
        guard[i] = [dd, a3]
    stk = s.build()
    return nc, stk



_CACHE = {}


def _bfview(a):
    return a if a.dtype == ml_dtypes.bfloat16 else a.view(ml_dtypes.bfloat16)


def _get(name, builder):
    if name not in _CACHE:
        _CACHE[name] = builder()
    return _CACHE[name]


def kernel(L0_attn_hidden_states, L0_attn_in_proj_w, L0_attn_conv_w, L0_attn_conv_b,
           L0_attn_x_proj_w, L0_attn_dt_proj_w, L0_attn_A_log, L0_attn_D, L0_attn_out_proj_w,
           L0_ffn_Wc, L0_ffn_bc, L0_ffn_W1, L0_ffn_b1, L0_ffn_W2, L0_ffn_b2):
    f32 = np.float32
    bf16 = ml_dtypes.bfloat16
    cont = np.ascontiguousarray
    x = np.asarray(L0_attn_hidden_states, f32)
    W_in = np.asarray(L0_attn_in_proj_w, f32)
    conv_w = np.asarray(L0_attn_conv_w, f32)
    conv_b = np.asarray(L0_attn_conv_b, f32)
    xp_w = np.asarray(L0_attn_x_proj_w, f32)
    dtp_w = np.asarray(L0_attn_dt_proj_w, f32)
    A_log = np.asarray(L0_attn_A_log, f32)
    Dv = np.asarray(L0_attn_D, f32)
    W_out = np.asarray(L0_attn_out_proj_w, f32)
    Wc = np.asarray(L0_ffn_Wc, f32).reshape(DM, NES)
    bcv = np.asarray(L0_ffn_bc, f32).reshape(NES, 1)
    W1 = np.asarray(L0_ffn_W1, f32).reshape(NES, DM, 32)
    b1 = np.asarray(L0_ffn_b1, f32).reshape(NES, 32)
    W2 = np.asarray(L0_ffn_W2, f32).reshape(NES, DM, 32)
    b2 = np.asarray(L0_ffn_b2, f32).reshape(NES, DM)

    identb = np.eye(128, dtype=bf16)
    tie = np.linspace(0.0, 1e-6, GRP, dtype=f32)
    tie_full = cont(np.broadcast_to(np.tile(tie, NGRP)[None, :], (128, LH)))
    b1p = cont(b1.reshape(32, 128).T)
    w2T = cont(W2.transpose(0, 2, 1))
    W_inT_f = cont(W_in.T)
    xpT_f = cont(xp_w.T)
    dtpT_f = cont(dtp_w.T)
    WoT_f = cont(W_out.T)

    in1, in2_base, in3 = [], [], []
    for c in range(8):
        b, h = c // 2, c % 2
        sl = slice(DH * h, DH * h + DH)
        w_inT_c = cont(np.concatenate([W_inT_f[:, DH * h:DH * h + DH],
                                       W_inT_f[:, 2048 + DH * h:2048 + DH * h + DH]], axis=1))
        in1.append(dict(
            x_tm=cont(x[b]), w_inT=w_inT_c,
            convw=cont(conv_w[sl]), convb=cont(conv_b[sl].reshape(8, 128).T),
            xp_T=cont(xpT_f[sl]), identb=identb))
        in2_base.append(dict(
            dtpT=cont(dtpT_f[:, sl]), alog=cont(A_log[sl]),
            dvec=cont(Dv[sl].reshape(8, 128).T), woutT=cont(WoT_f[sl]),
            x_own=cont(x[b, LH * h:LH * h + LH]),
            wc=Wc, bc=bcv, tie=tie_full, w1=W1, b1p=b1p, w2T=w2T, b2=b2,
            identb=identb))

    nc1, _ = _get("L1", build_L1)
    r1 = run_bass_kernel_spmd(nc1, in1, core_ids=list(range(8)))

    in2 = []
    for c in range(8):
        pc = c ^ 1
        d = dict(in2_base[c])
        d["xbf"] = np.asarray(r1.results[c]["o_xbf"]).view(bf16) \
            if np.asarray(r1.results[c]["o_xbf"]).dtype != bf16 else np.asarray(r1.results[c]["o_xbf"])
        d["szbf"] = np.asarray(r1.results[c]["o_szbf"]).view(bf16) \
            if np.asarray(r1.results[c]["o_szbf"]).dtype != bf16 else np.asarray(r1.results[c]["o_szbf"])
        d["xdbl0"] = np.asarray(r1.results[c]["o_xdbl"])
        d["xdbl1"] = np.asarray(r1.results[pc]["o_xdbl"])
        in2.append(d)

    nc2, _ = _get("L2", build_L2)
    r2 = run_bass_kernel_spmd(nc2, in2, core_ids=list(range(8)))

    in3 = []
    for c in range(8):
        b, h = c // 2, c % 2
        pc = c ^ 1
        in3.append(dict(
            x_own=cont(x[b, LH * h:LH * h + LH]),
            attn_a=cont(_bfview(np.asarray(r2.results[c]["o_attn"]))[LH * h:LH * h + LH]),
            attn_b=cont(_bfview(np.asarray(r2.results[pc]["o_attn"]))[LH * h:LH * h + LH]),
            ffn=np.asarray(r2.results[c]["o_ffn"])))
    nc3, _ = _get("L3", build_L3)
    r3 = run_bass_kernel_spmd(nc3, in3, core_ids=list(range(8)))

    out = np.empty((4, L, DM), f32)
    for c in range(8):
        b, h = c // 2, c % 2
        out[b, LH * h:LH * h + LH] = np.asarray(r3.results[c]["o"])
    return out


# version 12
# speedup vs baseline: 1.0401x; 1.0401x over previous
import contextlib
import numpy as np
import ml_dtypes
import concourse.bass as bass
import concourse.mybir as mybir
from concourse.bass_utils import run_bass_kernel_spmd

F32 = mybir.dt.float32
BF16 = mybir.dt.bfloat16
AF = mybir.ActivationFunctionType
ALU = mybir.AluOpType
AX = mybir.AxisListType.X

ENGINES = ("sync", "tensor", "vector", "scalar", "gpsimd")

DM = 1024
DH = 1024
L = 4096
LH = 2048
NST = 16
DTR = 64
NES = 128
GRP = 32
NGRP = LH // GRP
LQ = 1024
DEBUG_L2 = False


class Sched:

    paranoid = False

    def __init__(self, nc):
        self.nc = nc
        self.ops = []
        self.pending = {e: [] for e in ENGINES}
        self.last_in_stream = {}

    def op(self, eng, fn, deps=(), hard=(), dma=False):
        deps = [d for d in deps if d is not None] + self.pending[eng]
        if self.paranoid:
            for j in range(len(self.ops) - 1, -1, -1):
                if self.ops[j]["eng"] == eng and self.ops[j]["stream"] is None:
                    deps.append(j)
                    break
        stream = None
        if dma:
            stream = dma if isinstance(dma, str) else "dma"
            prev = self.last_in_stream.get(stream)
            if prev is not None:
                deps.append(prev)
        i = len(self.ops)
        self.ops.append(dict(eng=eng, fn=fn, stream=stream, mark=bool(dma),
                             deps=tuple(deps),
                             hard=tuple(d for d in hard if d is not None)))
        if stream is not None:
            self.last_in_stream[stream] = i
        return i

    def barrier(self):
        last = {}
        for i, o in enumerate(self.ops):
            last[o["eng"] if o["stream"] is None else o["stream"]] = i
        deps = list(last.values())
        for e in ENGINES:
            self.pending[e] = list(deps)

    def build(self):
        nc = self.nc
        for o in self.ops:
            for d in o["deps"]:
                dop = self.ops[d]
                if dop["stream"] is not None or dop["eng"] != o["eng"] or self.paranoid:
                    dop["mark"] = True
            for d in o["hard"]:
                self.ops[d]["mark"] = True
        stk = contextlib.ExitStack()
        keys = list(ENGINES) + sorted({o["stream"] for o in self.ops if o["stream"]})
        sems = {k: stk.enter_context(nc.semaphore("sem_" + k)) for k in keys}
        val = {k: 0 for k in keys}
        for o in self.ops:
            k = o["stream"] or o["eng"]
            if o["mark"]:
                val[k] += 16 if o["stream"] else 1
            o["val"] = val[k]
            o["key"] = k
        final = dict(val)

        with nc.Block() as block:
            for eng_name in ENGINES:
                def mk(e):
                    def body(engine):
                        waited = {}
                        for o in self.ops:
                            if o["eng"] != e:
                                continue
                            for d in tuple(o["deps"]) + tuple(o["hard"]):
                                dop = self.ops[d]
                                tk, tv = dop["key"], dop["val"]
                                if (dop["stream"] is None and dop["eng"] == e
                                        and d not in o["hard"] and not self.paranoid):
                                    continue
                                if waited.get(tk, 0) < tv:
                                    engine.wait_ge(sems[tk], tv)
                                    waited[tk] = tv
                            inst = o["fn"](engine)
                            if o["mark"]:
                                inst.then_inc(sems[o["key"]], 16 if o["stream"] else 1)
                        if e == "sync":
                            for tk in keys:
                                if tk != "sync" and final[tk] > waited.get(tk, 0):
                                    engine.wait_ge(sems[tk], final[tk])
                    return body
                getattr(block, eng_name)(mk(eng_name))
        return stk


def _ln_chunk(s, xb, xout, stat, sqs, dep_load, prev, cvals, cdep, xout_guard=None):
    m2_prev, sqs_prev = prev
    r1 = s.op("vector", lambda e: e.reduce_sum(stat[:, 0:1], xb[:], AX), deps=[dep_load])
    r2 = s.op("scalar", lambda e: e.activation(sqs[:], xb[:], AF.Square, accum_out=stat[:, 1:2]),
              deps=[dep_load, m2_prev, sqs_prev])
    m1 = s.op("vector", lambda e: e.tensor_scalar(stat[:, 2:3], stat[:, 0:1], 1.0 / DM, None, ALU.mult),
              deps=[r2], hard=[r1])
    m2 = s.op("vector", lambda e: e.tensor_scalar(stat[:, 3:4], stat[:, 1:2], 1.0 / DM, None, ALU.mult))
    m3 = s.op("vector", lambda e: e.tensor_tensor(stat[:, 4:5], stat[:, 2:3], stat[:, 2:3], ALU.mult),
              hard=[m1])
    m4 = s.op("vector", lambda e: e.tensor_tensor(stat[:, 5:6], stat[:, 3:4], stat[:, 4:5], ALU.subtract),
              hard=[m2, m3])
    v1 = s.op("scalar", lambda e: e.activation(stat[:, 6:7], stat[:, 5:6], AF.Ln, bias=cvals[:, 0:1]),
              deps=[m4, cdep])
    v2 = s.op("scalar", lambda e: e.activation(stat[:, 7:8], stat[:, 6:7], AF.Exp, scale=cvals[:, 1:2]),
              hard=[v1])
    xn = s.op("vector", lambda e: e.tensor_scalar(
        xout[:], xb[:], stat[:, 2:3], stat[:, 7:8], ALU.subtract, ALU.mult),
        deps=[v2, xout_guard], hard=[m1])
    return xn, r2, m2



def build_L1():
    nc = bass.Bass("TRN2", target_bir_lowering=False)
    x_tm = nc.dram_tensor("x_tm", [L, DM], F32, kind="ExternalInput").ap()
    w_inT = nc.dram_tensor("w_inT", [DM, 2 * DH], F32, kind="ExternalInput").ap()
    convw = nc.dram_tensor("convw", [DH, 4], F32, kind="ExternalInput").ap()
    convb = nc.dram_tensor("convb", [128, 8], F32, kind="ExternalInput").ap()
    xp_T = nc.dram_tensor("xp_T", [DH, 96], F32, kind="ExternalInput").ap()
    identb_d = nc.dram_tensor("identb", [128, 128], BF16, kind="ExternalInput").ap()

    o_xbf = nc.dram_tensor("o_xbf", [DH, L], BF16, kind="ExternalOutput").ap()
    o_szbf = nc.dram_tensor("o_szbf", [DH, L], BF16, kind="ExternalOutput").ap()
    o_xdbl = nc.dram_tensor("o_xdbl", [96, L], F32, kind="ExternalOutput").ap()

    s = Sched(nc)
    sb = nc.alloc_sbuf_tensor
    ps = nc.alloc_psum_tensor

    ident = sb("ident", [128, 128], BF16)
    xn_fm = [sb(f"xn_fm{k}", [128, L], BF16) for k in range(8)]
    w_stage = [sb(f"wst{i}", [128, DH], F32) for i in range(2)]
    w_bf = [sb(f"wbf{k}", [128, 2 * DH], BF16) for k in range(8)]
    cw = sb("cw", [128, 32], F32)
    cb = sb("cb", [128, 8], F32)
    xpt = sb("xpt", [128, 8 * 96], F32)
    xptb = sb("xptb", [128, 8 * 96], BF16)
    xch = [sb(f"xch{i}", [128, DM], F32) for i in range(2)]
    stat = [sb(f"stat{i}", [128, 8], F32) for i in range(2)]
    sqs = [sb(f"sqs{i}", [128, DM], F32) for i in range(2)]
    cvals = sb("cvals", [128, 3], F32)
    xnc = [sb(f"xnc{i}", [128, DM], BF16) for i in range(2)]
    xc = [sb(f"xc{r}", [128, 3 + LQ], F32) for r in range(8)]
    cacc = sb("cacc", [128, LQ], F32)
    cacc2 = [sb(f"cacc2{i}", [128, LQ], F32) for i in range(2)]
    xbf_t = [sb(f"xbf{r}", [128, LQ], BF16) for r in range(8)]
    szbf_t = [sb(f"szbf{i}", [128, LQ], BF16) for i in range(2)]
    xdbl_sb = [sb(f"xdbl_sb{i}", [96, LQ], F32) for i in range(2)]

    tpp = [ps(f"tpp{i}", [128, DM], BF16) for i in range(2)]
    ipp = [ps(f"ipp{i}", [128, 512], F32) for i in range(4)]
    xpp = ps("xpp", [96, LQ], F32)

    d_ident = s.op("sync", lambda e: e.dma_start(out=ident[:], in_=identb_d), dma="const")
    d_cw = s.op("sync", lambda e: e.dma_start(
        out=cw[:].rearrange("p (d k) -> p d k", k=4),
        in_=convw.rearrange("(d p) k -> p d k", p=128)), dma="const")
    d_cb = s.op("sync", lambda e: e.dma_start(out=cb[:], in_=convb), dma="const")
    d_xpt = s.op("sync", lambda e: e.dma_start(
        out=xpt[:].rearrange("p (d n) -> p d n", n=96),
        in_=xp_T.rearrange("(d p) n -> p d n", p=128)), dma="const")
    c_xpt = s.op("vector", lambda e: e.tensor_copy(xptb[:], xpt[:]), deps=[d_xpt])
    cv1 = s.op("vector", lambda e: e.memset(cvals[:, 0:1], 1e-5))
    cv2 = s.op("vector", lambda e: e.memset(cvals[:, 1:2], -0.5))
    cdep = s.op("vector", lambda e: e.memset(cvals[:, 2:3], 1.0))

    wcasts = []
    for kk in range(16):
        k, half = kk // 2, kk % 2
        st = w_stage[kk % 2]
        d = s.op("sync", lambda e, st=st, k=k, half=half: e.dma_start(
            out=st[:], in_=w_inT[128 * k:128 * (k + 1), DH * half:DH * half + DH]),
            deps=[wcasts[kk - 2] if kk >= 2 else None], dma="win")
        wcasts.append(s.op("vector", lambda e, st=st, k=k, half=half: e.tensor_copy(
            w_bf[k][:, DH * half:DH * half + DH], st[:]), deps=[d]))

    ev_prev = {}
    ln_prev = {}
    reader_prev = {}
    tr_last = {}
    for c in range(L // 128):
        xb = xch[c % 2]
        d = s.op("sync", lambda e, xb=xb, c=c: e.dma_start(
            out=xb[:], in_=x_tm[128 * c:128 * (c + 1), :]),
            deps=reader_prev.get(c % 2, []), dma="xin")
        xn, r2, m2 = _ln_chunk(s, xb, xnc[c % 2], stat[c % 2], sqs[c % 2], d,
                               ln_prev.get(c % 2, (None, None)), cvals, cdep,
                               xout_guard=tr_last.get(c % 2))
        ln_prev[c % 2] = (m2, None)
        reader_prev[c % 2] = [xn, r2]
        for k in range(8):
            pb = (c * 8 + k) % 2
            t = s.op("tensor", lambda e, pb=pb, k=k, c=c: e.transpose(
                tpp[pb][:, 0:128], xnc[c % 2][:, 128 * k:128 * (k + 1)], ident[:]),
                deps=[xn, d_ident, ev_prev.get(pb)])
            tr_last[c % 2] = t
            ev_prev[pb] = s.op("vector", lambda e, pb=pb, k=k, c=c: e.tensor_copy(
                xn_fm[k][:, 128 * c:128 * (c + 1)], tpp[pb][:, 0:128]), deps=[t])
    ln_done = list(ev_prev.values())

    czero = [s.op("vector", lambda e, r=r: e.memset(xc[r][:, 0:3], 0.0)) for r in range(8)]

    xproj_prev = None
    psum_guard = {}
    xbf_guard = {}
    szdma_prev = {}
    xc_guard = {}
    sil_par = {}
    sils, dxs = {}, {}
    for lc in range(4):
        xproj_mms = []
        pend_xp = []

        def emit_xproj(dc):
            xms = []
            for nh in range(2):
                xm = s.op("tensor", lambda e, dc=dc, nh=nh: e.matmul(
                    xpp[0:96, 512 * nh:512 * nh + 512],
                    xptb[:, 96 * dc:96 * (dc + 1)],
                    xbf_t[dc][:, 512 * nh:512 * nh + 512],
                    start=(dc == 0), stop=(dc == 7)),
                    deps=[sils[dc], c_xpt] + ([xproj_prev] if (dc == 0 and nh == 0) else []))
                xms.append(xm)
                xproj_mms.append(xm)
            xbf_guard[dc] = [dxs[dc]] + xms
        for rc in range(16):
            pp = ipp[rc % 4]
            evs = []
            for nh in range(2):
                last_mm = None
                dep0 = [wcasts[15], psum_guard.get(rc % 4)] + (ln_done if lc == 0 else [])
                if nh == 1:
                    dep0 = [evs[0]]
                for k in range(8):
                    last_mm = s.op("tensor", lambda e, pp=pp, k=k, nh=nh, rc=rc, lc=lc: e.matmul(
                        pp[:],
                        w_bf[k][:, 128 * rc:128 * (rc + 1)],
                        xn_fm[k][:, LQ * lc + 512 * nh:LQ * lc + 512 * nh + 512],
                        start=(k == 0), stop=(k == 7)),
                        deps=(dep0 if k == 0 else []))
                if rc < 8:
                    ev = s.op("scalar", lambda e, pp=pp, rc=rc, nh=nh: e.copy(
                        xc[rc][:, 3 + 512 * nh:3 + 512 * nh + 512], pp[:]),
                        deps=[last_mm, xc_guard.get(rc)])
                else:
                    ev = s.op("scalar", lambda e, pp=pp, rc=rc, nh=nh: e.activation(
                        szbf_t[rc % 2][:, 512 * nh:512 * nh + 512], pp[:], AF.Silu),
                        deps=[last_mm, szdma_prev.get(rc % 2)])
                evs.append(ev)
            psum_guard[rc % 4] = evs[1]
            if rc < 8:
                dc = rc
                a0 = s.op("vector", lambda e, dc=dc: e.tensor_scalar(
                    cacc[:], xc[dc][:, 0:LQ], cw[:, 4 * dc:4 * dc + 1], None, ALU.mult),
                    deps=[evs[1], d_cw, czero[dc]])
                cc2 = cacc2[dc % 2]
                a1 = s.op("vector", lambda e, dc=dc, cc2=cc2: e.scalar_tensor_tensor(
                    cc2[:], xc[dc][:, 1:1 + LQ], cw[:, 4 * dc + 1:4 * dc + 2], cacc[:],
                    ALU.mult, ALU.add), deps=[sil_par.get(dc % 2)])
                a2 = s.op("vector", lambda e, dc=dc, cc2=cc2: e.scalar_tensor_tensor(
                    cacc[:], xc[dc][:, 2:2 + LQ], cw[:, 4 * dc + 2:4 * dc + 3], cc2[:],
                    ALU.mult, ALU.add))
                a3 = s.op("vector", lambda e, dc=dc, cc2=cc2: e.scalar_tensor_tensor(
                    cc2[:], xc[dc][:, 3:3 + LQ], cw[:, 4 * dc + 3:4 * dc + 4], cacc[:],
                    ALU.mult, ALU.add))
                ccp = s.op("vector", lambda e, dc=dc: e.tensor_copy(xc[dc][:, 0:3], xc[dc][:, LQ:LQ + 3]),
                           deps=[a3])
                xc_guard[dc] = ccp
                sil = s.op("scalar", lambda e, dc=dc, cc2=cc2: e.activation(
                    xbf_t[dc][:], cc2[:], AF.Silu, bias=cb[:, dc:dc + 1]),
                    deps=[a3, d_cb] + xbf_guard.get(dc, []))
                sil_par[dc % 2] = sil
                dx = s.op("sync", lambda e, dc=dc, lc=lc: e.dma_start(
                    out=o_xbf[128 * dc:128 * (dc + 1), LQ * lc:LQ * (lc + 1)], in_=xbf_t[dc][:]),
                    deps=[sil], dma="oxbf")
                sils[dc] = sil
                dxs[dc] = dx
                pend_xp.append(dc)
                if len(pend_xp) > 2:
                    emit_xproj(pend_xp.pop(0))
            else:
                zi = rc - 8
                dz = s.op("sync", lambda e, zi=zi, lc=lc: e.dma_start(
                    out=o_szbf[128 * zi:128 * (zi + 1), LQ * lc:LQ * (lc + 1)], in_=szbf_t[zi % 2][:]),
                    deps=[evs[1]], dma="oszbf")
                szdma_prev[zi % 2] = dz
        while pend_xp:
            emit_xproj(pend_xp.pop(0))
        xev = s.op("vector", lambda e, lc=lc: e.tensor_copy(
            xdbl_sb[lc % 2][0:96, :], xpp[0:96, :]),
            deps=[xproj_mms[-1], xproj_prev])
        xproj_prev = s.op("sync", lambda e, lc=lc: e.dma_start(
            out=o_xdbl[:, LQ * lc:LQ * (lc + 1)], in_=xdbl_sb[lc % 2][0:96, :]),
            deps=[xev], dma="oxdbl")
    stk = s.build()
    return nc, stk



def build_L2():
    nc = bass.Bass("TRN2", target_bir_lowering=False)
    xbf_d = nc.dram_tensor("xbf", [DH, L], BF16, kind="ExternalInput").ap()
    szbf_d = nc.dram_tensor("szbf", [DH, L], BF16, kind="ExternalInput").ap()
    xdbl0_d = nc.dram_tensor("xdbl0", [96, L], F32, kind="ExternalInput").ap()
    xdbl1_d = nc.dram_tensor("xdbl1", [96, L], F32, kind="ExternalInput").ap()
    dtpT_d = nc.dram_tensor("dtpT", [DTR, DH], F32, kind="ExternalInput").ap()
    alog_d = nc.dram_tensor("alog", [DH, NST], F32, kind="ExternalInput").ap()
    dvec_d = nc.dram_tensor("dvec", [128, 8], F32, kind="ExternalInput").ap()
    woutT_d = nc.dram_tensor("woutT", [DH, DM], F32, kind="ExternalInput").ap()
    x_own_d = nc.dram_tensor("x_own", [LH, DM], F32, kind="ExternalInput").ap()
    wc_d = nc.dram_tensor("wc", [DM, NES], F32, kind="ExternalInput").ap()
    bc_d = nc.dram_tensor("bc", [NES, 1], F32, kind="ExternalInput").ap()
    tie_d = nc.dram_tensor("tie", [128, LH], F32, kind="ExternalInput").ap()
    w1_d = nc.dram_tensor("w1", [NES, DM, 32], F32, kind="ExternalInput").ap()
    b1p_d = nc.dram_tensor("b1p", [128, 32], F32, kind="ExternalInput").ap()
    w2T_d = nc.dram_tensor("w2T", [NES, 32, DM], F32, kind="ExternalInput").ap()
    b2_d = nc.dram_tensor("b2", [NES, DM], F32, kind="ExternalInput").ap()
    identb_d = nc.dram_tensor("identb", [128, 128], BF16, kind="ExternalInput").ap()

    o_attn = nc.dram_tensor("o_attn", [L, DM], BF16, kind="ExternalOutput").ap()
    o_ffn = nc.dram_tensor("o_ffn", [LH, DM], F32, kind="ExternalOutput").ap()
    bc_dram = nc.dram_tensor("bc_dram", [32, L], BF16, kind="Internal").ap()
    if DEBUG_L2:
        o_mask = nc.dram_tensor("o_mask", [128, LH], BF16, kind="ExternalOutput").ap()
        o_logit = nc.dram_tensor("o_logit", [128, LH], F32, kind="ExternalOutput").ap()
        o_xot = nc.dram_tensor("o_xot", [LH, DM], BF16, kind="ExternalOutput").ap()
        o_perm = nc.dram_tensor("o_perm", [128, 8 * 64 * 32], BF16, kind="ExternalOutput").ap()

    s = Sched(nc)
    sb = nc.alloc_sbuf_tensor

    ident = sb("ident", [128, 128], BF16)
    xdbl = sb("xdbl", [96, L], F32)
    xot = [sb(f"xot{c}", [128, DM], BF16) for c in range(16)]
    mask_all = sb("mask_all", [128, LH], BF16)
    maskT = sb("maskT", [128, (NGRP // 4) * 128], BF16)
    amat = sb("amat", [128, 8 * NST], F32)
    dvec = sb("dvec_sb", [128, 8], F32)
    dtpb = sb("dtpb", [DTR, DH], F32)
    carry = [sb(f"carry{r}", [128, NST], BF16) for r in range(8)]
    b2b = sb("b2b", [128, DM], BF16)
    b1pk = sb("b1pk", [128, 32], F32)
    cvals = sb("cvals", [128, 3], F32)

    d_ident = s.op("sync", lambda e: e.dma_start(out=ident[:], in_=identb_d), dma="const")
    cv1 = s.op("vector", lambda e: e.memset(cvals[:, 0:1], 1e-5))
    cv2 = s.op("vector", lambda e: e.memset(cvals[:, 1:2], -0.5))
    cdep = s.op("vector", lambda e: e.memset(cvals[:, 2:3], 1.0))
    da = s.op("sync", lambda e: e.dma_start(
        out=amat[:].rearrange("p (d n) -> p d n", n=NST),
        in_=alog_d.rearrange("(d p) n -> p d n", p=128)), dma="const")
    ae = s.op("scalar", lambda e: e.activation(amat[:], amat[:], AF.Exp), deps=[da])
    an = s.op("vector", lambda e: e.tensor_scalar(amat[:], amat[:], -1.0, None, ALU.mult), deps=[ae])
    dd = s.op("sync", lambda e: e.dma_start(out=dvec[:], in_=dvec_d), dma="const")
    ddtp = s.op("sync", lambda e: e.dma_start(out=dtpb[0:DTR, :], in_=dtpT_d), dma="const")
    db1 = s.op("sync", lambda e: e.dma_start(out=b1pk[:], in_=b1p_d), dma="const")

    A = contextlib.ExitStack()
    sbA = lambda *a: A.enter_context(nc.sbuf_tensor(*a))
    psA = lambda *a: A.enter_context(nc.psum_tensor(*a))
    xdbl_p1 = sbA("xdbl_p1", [96, L], F32)
    stat = [sbA(f"stat{i}", [128, 8], F32) for i in range(2)]
    sqs = [sbA(f"sqs{i}", [128, DM], F32) for i in range(2)]
    xch = [sbA(f"xch{i}", [128, DM], F32) for i in range(2)]
    b2stg = sbA("b2stg", [128, DM], F32)
    xdbl_bc = sbA("xdbl_bc", [32, L], BF16)
    xof_hi = [sbA(f"xofh{k}", [128, LH], BF16) for k in range(8)]
    xof_lo = [sbA(f"xofl{k}", [128, LH], BF16) for k in range(8)]
    logit = sbA("logit", [128, LH], F32)
    mxa = sbA("mxa", [128, NGRP], F32)
    ties = sbA("ties", [128, LH], F32)
    wcs = sbA("wcs", [128, 8 * NES], F32)
    wch = sbA("wch", [128, 8 * NES], BF16)
    wcl = sbA("wcl", [128, 8 * NES], BF16)
    wclf = sbA("wclf", [128, 8 * NES], F32)
    bcs = sbA("bcs", [128, 1], F32)
    xlo_c = [sbA(f"xlo{i}", [128, DM], BF16) for i in range(2)]
    tppA = [psA(f"tppA{i}", [128, DM], BF16) for i in range(2)]
    lgp = [psA(f"lgp{i}", [128, 512], F32) for i in range(2)]
    mtp = psA("mtp", [128, 128], BF16)

    db2 = s.op("sync", lambda e: e.dma_start(out=b2stg[:], in_=b2_d), dma="const")
    cb2 = s.op("vector", lambda e: e.tensor_copy(b2b[:], b2stg[:]), deps=[db2])
    d0 = s.op("sync", lambda e: e.dma_start(out=xdbl[0:96, :], in_=xdbl0_d), dma="xdl")
    d1 = s.op("sync", lambda e: e.dma_start(out=xdbl_p1[0:96, :], in_=xdbl1_d), dma="xdl")
    xsum = s.op("vector", lambda e: e.tensor_add(xdbl[0:96, :], xdbl[0:96, :], xdbl_p1[0:96, :]),
                deps=[d0, d1])
    xcast = s.op("vector", lambda e: e.tensor_copy(xdbl_bc[0:32, :], xdbl[64:96, :]), deps=[xsum])
    dbc = s.op("sync", lambda e: e.dma_start(out=bc_dram, in_=xdbl_bc[0:32, :]), deps=[xcast], dma="xdl")

    dwc = s.op("sync", lambda e: e.dma_start(
        out=wcs[:].rearrange("p (d n) -> p d n", n=NES),
        in_=wc_d.rearrange("(d p) n -> p d n", p=128)), dma="const")
    dbcb = s.op("sync", lambda e: e.dma_start(out=bcs[:], in_=bc_d), dma="const")
    dtie = s.op("sync", lambda e: e.dma_start(out=ties[:], in_=tie_d), dma="const")
    ch1 = s.op("vector", lambda e: e.tensor_copy(wch[:], wcs[:]), deps=[dwc])
    ch2 = s.op("vector", lambda e: e.tensor_sub(wclf[:], wcs[:], wch[:]))
    ch3 = s.op("vector", lambda e: e.tensor_copy(wcl[:], wclf[:]))

    ev_prev = {}
    ln_prev = {}
    reader_prev = {}
    t2_last = {}
    for c in range(16):
        xb = xch[c % 2]
        d = s.op("sync", lambda e, xb=xb, c=c: e.dma_start(
            out=xb[:], in_=x_own_d[128 * c:128 * (c + 1), :]),
            deps=reader_prev.get(c % 2, []), dma="xin")
        st_, sq_ = stat[c % 2], sqs[c % 2]
        xn, r2, m2 = _ln_chunk(s, xb, xot[c], st_, sq_, d, ln_prev.get(c % 2, (None, None)),
                               cvals, cdep)
        lo1 = s.op("vector", lambda e, xb=xb, st_=st_, sq_=sq_: e.tensor_scalar(
            sq_[:], xb[:], st_[:, 2:3], st_[:, 7:8], ALU.subtract, ALU.mult), hard=[xn], deps=[r2])
        lo2 = s.op("vector", lambda e, c=c, sq_=sq_: e.tensor_sub(xlo_c[c % 2][:], sq_[:], xot[c][:]),
                   deps=[t2_last.get(c % 2)])
        ln_prev[c % 2] = (m2, lo2)
        reader_prev[c % 2] = [lo2, r2]
        for k in range(8):
            pb = (c * 8 + k) % 2
            t1 = s.op("tensor", lambda e, pb=pb, k=k, c=c: e.transpose(
                tppA[pb][:, 0:128], xot[c][:, 128 * k:128 * (k + 1)], ident[:]),
                deps=[xn, d_ident, ev_prev.get(pb)])
            e1 = s.op("vector", lambda e, pb=pb, k=k, c=c: e.tensor_copy(
                xof_hi[k][:, 128 * c:128 * (c + 1)], tppA[pb][:, 0:128]), deps=[t1])
            t2 = s.op("tensor", lambda e, pb=pb, k=k, c=c: e.transpose(
                tppA[pb][:, 128:256], xlo_c[c % 2][:, 128 * k:128 * (k + 1)], ident[:]),
                deps=[lo2, e1])
            t2_last[c % 2] = t2
            ev_prev[pb] = s.op("vector", lambda e, pb=pb, k=k, c=c: e.tensor_copy(
                xof_lo[k][:, 128 * c:128 * (c + 1)], tppA[pb][:, 128:256]), deps=[t2])
    ln_done = list(ev_prev.values())

    lg_ev = []
    for q in range(LH // 512):
        pp = lgp[q % 2]
        mm = None
        for p3 in range(3):
            lhs, rhs = [(wch, xof_hi), (wch, xof_lo), (wcl, xof_hi)][p3]
            for k in range(8):
                mm = s.op("tensor", lambda e, pp=pp, k=k, q=q, lhs=lhs, rhs=rhs, p3=p3: e.matmul(
                    pp[:], lhs[:, NES * k:NES * (k + 1)], rhs[k][:, 512 * q:512 * q + 512],
                    start=(p3 == 0 and k == 0), stop=(p3 == 2 and k == 7)),
                    deps=(([ch1, ch3] + ln_done + [lg_ev[q - 2] if q >= 2 else None])
                          if (p3 == 0 and k == 0) else []))
        lg_ev.append(s.op("scalar", lambda e, pp=pp, q=q: e.activation(
            logit[:, 512 * q:512 * q + 512], pp[:], AF.Identity, bias=bcs[:]), deps=[mm, dbcb]))
    tie_add = s.op("vector", lambda e: e.tensor_add(logit[:], logit[:], ties[:]), deps=lg_ev + [dtie])
    red = s.op("vector", lambda e: e.tensor_reduce(
        mxa[:], logit[:].rearrange("p (g t) -> p g t", t=GRP), AX, ALU.max), deps=[tie_add])
    mk_ops = []
    for g in range(NGRP):
        mk_ops.append(s.op("vector", lambda e, g=g: e.tensor_scalar(
            mask_all[:, GRP * g:GRP * (g + 1)], logit[:, GRP * g:GRP * (g + 1)],
            mxa[:, g:g + 1], None, ALU.is_equal), hard=[red]))
    if DEBUG_L2:
        s.op("sync", lambda e: e.dma_start(out=o_logit, in_=logit[:]), deps=[red], dma="dbg")
    mt_prev = None
    for g in range(NGRP):
        t = s.op("tensor", lambda e, g=g: e.transpose(
            mtp[32 * (g % 4):32 * (g % 4) + 32, 0:128], mask_all[:, GRP * g:GRP * (g + 1)], ident[:],
            tile_position=(0, 32 * (g % 4))),
            deps=[mk_ops[g], mt_prev])
        mt_prev = s.op("vector", lambda e, g=g: e.tensor_copy(
            maskT[32 * (g % 4):32 * (g % 4) + 32, 128 * (g // 4):128 * (g // 4) + 128],
            mtp[32 * (g % 4):32 * (g % 4) + 32, 0:128]), deps=[t])

    if DEBUG_L2:
        s.op("sync", lambda e: e.dma_start(out=o_mask, in_=mask_all[:]), deps=mk_ops, dma="dbg")
        for c in range(16):
            s.op("sync", lambda e, c=c: e.dma_start(
                out=o_xot[128 * c:128 * (c + 1), :], in_=xot[c][:]), deps=[mt_prev], dma="dbg")
    A.close()
    s.barrier()

    B = contextlib.ExitStack()
    sbB = lambda *a: B.enter_context(nc.sbuf_tensor(*a))
    psB = lambda *a: B.enter_context(nc.psum_tensor(*a))
    bbc = sbB("bbc", [128, NST * LQ], BF16)
    cbc = sbB("cbc", [128, NST * LQ], BF16)
    h_rot = [sbB(f"hrot{i}", [128, LQ], BF16) for i in range(2)]
    out_bf = [[sbB(f"obf{p}_{r}", [128, LQ], BF16) for r in range(8)] for p in range(2)]
    woT = [sbB(f"wot{k}", [128, DM], BF16) for k in range(8)]
    wost = [sbB("wost0", [128, 256], F32)] * 2
    xbf_t = [sbB(f"xbft{i}", [128, LQ], BF16) for i in range(2)]
    szb_t = [sbB(f"szbt{i}", [128, LQ], BF16) for i in range(2)]
    delt = [sbB(f"delt{i}", [128, LQ], BF16) for i in range(2)]
    delx = [sbB(f"delx{i}", [128, LQ], BF16) for i in range(2)]
    ebuf = [sbB(f"ebuf{i}", [128, LQ], BF16) for i in range(2)]
    wbuf = sbB("wbuf", [128, LQ], BF16)
    pbuf = [sbB(f"pbuf{i}", [128, LQ], BF16) for i in range(2)]
    at_sb = [sbB("atsb0", [128, 512], BF16)] * 2
    dt_ps = psB("dt_ps", [128, LQ], F32)
    y_ps = psB("y_ps", [128, LQ], F32)
    at_ps = [psB(f"at_ps{i}", [128, 512], F32) for i in range(2)]

    wo_casts = []
    for kk in range(32):
        k, half = kk // 4, kk % 4
        st = wost[kk % 2]
        d = s.op("sync", lambda e, st=st, k=k, half=half: e.dma_start(
            out=st[:], in_=woutT_d[128 * k:128 * (k + 1), 256 * half:256 * half + 256]),
            deps=[wo_casts[kk - 1] if kk >= 1 else None], dma="wo")
        wo_casts.append(s.op("vector", lambda e, st=st, k=k, half=half: e.tensor_copy(
            woT[k][:, 256 * half:256 * half + 256], st[:]), deps=[d]))
    zinit = [s.op("vector", lambda e, r=r: e.memset(carry[r][:], 0.0)) for r in range(8)]

    bn_readers, cn_readers = {}, {}
    x_readers = {}
    chains = {}

    def emit_chain(g):
        lq, dc = divmod(g, 8)
        dci = g % 2
        dxt = s.op("sync", lambda e, dc=dc, lq=lq, dci=dci: e.dma_start(
            out=xbf_t[dci][:], in_=xbf_d[128 * dc:128 * (dc + 1), LQ * lq:LQ * (lq + 1)]),
            deps=x_readers.get(dci, []), dma="xl")
        dst = s.op("sync", lambda e, dc=dc, lq=lq, dci=dci: e.dma_start(
            out=szb_t[dci][:], in_=szbf_d[128 * dc:128 * (dc + 1), LQ * lq:LQ * (lq + 1)]),
            deps=x_readers.get(dci, []), dma="szl")
        dtm = None
        for nh in range(2):
            dtm = s.op("tensor", lambda e, dc=dc, lq=lq, nh=nh: e.matmul(
                dt_ps[:, 512 * nh:512 * nh + 512],
                dtpb[0:DTR, 128 * dc:128 * (dc + 1)],
                xdbl[0:DTR, LQ * lq + 512 * nh:LQ * lq + 512 * nh + 512],
                start=True, stop=True),
                deps=[xsum, ddtp, state["dtexp"]] if nh == 0 else [])
        ex = s.op("scalar", lambda e, dci=dci: e.activation(delx[dci][:], dt_ps[:], AF.Exp),
                  deps=[dtm] + x_readers.get(dci, []))
        state["dtexp"] = ex
        splus = s.op("scalar", lambda e, dci=dci: e.activation(
            delt[dci][:], delx[dci][:], AF.Ln, bias=cvals[:, 2:3]),
            deps=[ex, cdep] + delt_guard.get(dci, []))
        dxm = s.op("vector", lambda e, dci=dci: e.tensor_mul(delx[dci][:], delt[dci][:], xbf_t[dci][:]),
                   deps=[splus, dxt])
        return (dxt, dst, splus, dxm)

    state = {"dtexp": None}
    gate_prev = None
    pending_opj = []
    outproj_guard = {}
    opj_gen = [None]

    def _opj_ops(lq, g2dep):
        for tc in range(8):
            pb = tc % 2
            for nh in range(2):
                mm = None
                for dc in range(8):
                    mm = s.op("tensor", lambda e, dc=dc, nh=nh, tc=tc, pb=pb, lq=lq: e.matmul(
                        at_ps[pb][:],
                        out_bf[lq % 2][dc][:, 128 * tc:128 * (tc + 1)],
                        woT[dc][:, 512 * nh:512 * nh + 512],
                        start=(dc == 0), stop=(dc == 7)),
                        deps=([g2dep, wo_casts[31], atev_prev.get(pb)] if dc == 0 else []))
                ev = s.op("vector", lambda e, pb=pb, nh=nh: e.tensor_copy(
                    at_sb[pb][:], at_ps[pb][:]),
                    deps=[mm, atdma_prev.get("last")])
                atev_prev[pb] = ev
                for dc in range(8):
                    outproj_guard[(lq % 2, dc)] = mm
                atdma_prev["last"] = s.op("sync", lambda e, pb=pb, tc=tc, lq=lq, nh=nh: e.dma_start(
                    out=o_attn[LQ * lq + 128 * tc:LQ * lq + 128 * (tc + 1), 512 * nh:512 * nh + 512],
                    in_=at_sb[pb][:]),
                    deps=[atev_prev[pb]], dma="oattn")
            yield

    def pump_outproj(lq, g2dep, all_now=False):
        if opj_gen[0] is None:
            opj_gen[0] = _opj_ops(lq, g2dep)
        try:
            while True:
                next(opj_gen[0])
                if not all_now:
                    return
        except StopIteration:
            opj_gen[0] = None

    ebuf_guard = {}
    hrot_guard = {}
    pbuf_guard = {}
    delt_guard = {}
    carry_guard = {}
    outproj_last = None
    atev_prev = {}
    atdma_prev = {}
    it = 0
    for lq in range(4):
        bld_b, bld_c = {}, {}
        for n in range(NST):
            bld_b[n] = s.op("sync", lambda e, n=n, lq=lq: e.dma_start(
                out=bbc[:, LQ * n:LQ * (n + 1)],
                in_=bc_dram[n:n + 1, LQ * lq:LQ * (lq + 1)].broadcast_to([128, LQ])),
                deps=[dbc, bn_readers.get(n)], dma="bl%d" % (n % 4))
            bld_c[n] = s.op("sync", lambda e, n=n, lq=lq: e.dma_start(
                out=cbc[:, LQ * n:LQ * (n + 1)],
                in_=bc_dram[NST + n:NST + n + 1, LQ * lq:LQ * (lq + 1)].broadcast_to([128, LQ])),
                deps=[dbc, cn_readers.get(n)], dma="cl%d" % (n % 4))
        for dc in range(8):
            g = lq * 8 + dc
            dci = g % 2
            if g == 0:
                chains[0] = emit_chain(0)
            ch = chains.pop(g)
            dxt, dst, splus, dxm = ch
            pm = None
            for n in range(NST):
                ib = it % 2
                eo = s.op("scalar", lambda e, ib=ib, dci=dci, dc=dc, n=n: e.activation(
                    ebuf[ib][:], delt[dci][:], AF.Exp, scale=amat[:, NST * dc + n:NST * dc + n + 1]),
                    deps=[splus, an, ebuf_guard.get(ib)])
                wo = s.op("vector", lambda e, dci=dci, n=n: e.tensor_mul(
                    wbuf[:], delx[dci][:], bbc[:, LQ * n:LQ * (n + 1)]),
                    deps=[dxm, bld_b[n]])
                sc = s.op("vector", lambda e, ib=ib, n=n, dc=dc: e.tensor_tensor_scan(
                    h_rot[ib][:], ebuf[ib][:], wbuf[:], carry[dc][:, n:n + 1], ALU.mult, ALU.add),
                    deps=[eo, wo, hrot_guard.get(ib), carry_guard.get((dc, n))], hard=[zinit[dc]])
                ebuf_guard[ib] = sc
                cu = s.op("gpsimd", lambda e, dc=dc, n=n, ib=ib: e.tensor_copy(
                    carry[dc][:, n:n + 1], h_rot[ib][:, LQ - 1:LQ]), deps=[sc])
                carry_guard[(dc, n)] = cu
                po = s.op("vector", lambda e, n=n, ib=ib: e.tensor_mul(
                    pbuf[ib][:], h_rot[ib][:], cbc[:, LQ * n:LQ * (n + 1)]),
                    deps=[sc, bld_c[n], pbuf_guard.get(ib)])
                if dc == 7:
                    bn_readers[n] = wo
                    cn_readers[n] = po
                hrot_guard[ib] = cu
                for nh in range(2):
                    pm = s.op("tensor", lambda e, n=n, nh=nh, ib=ib: e.matmul(
                        y_ps[:, 512 * nh:512 * nh + 512], ident[:],
                        pbuf[ib][:, 512 * nh:512 * nh + 512],
                        start=(n == 0), stop=(n == NST - 1)),
                        deps=[po, d_ident] + ([gate_prev] if (n == 0 and nh == 0) else []))
                pbuf_guard[ib] = pm
                it += 1
                if n == 1 and g + 1 < 32:
                    chains[g + 1] = emit_chain(g + 1)
            g1 = s.op("vector", lambda e, dci=dci, dc=dc: e.scalar_tensor_tensor(
                wbuf[:], xbf_t[dci][:], dvec[:, dc:dc + 1], y_ps[:], ALU.mult, ALU.add),
                deps=[pm, dd])
            g2 = s.op("vector", lambda e, dci=dci, dc=dc, lq=lq: e.tensor_mul(
                out_bf[lq % 2][dc][:], wbuf[:], szb_t[dci][:]),
                deps=[g1, dst, outproj_guard.get((lq % 2, dc))])
            gate_prev = g1
            delt_guard[dci] = [g1]
            x_readers[dci] = [dxm, g1, g2]
            for _ in range(2):
                if pending_opj and pending_opj[0][0] < lq:
                    pump_outproj(*pending_opj[0])
                    if opj_gen[0] is None:
                        pending_opj.pop(0)
        pending_opj.append((lq, g2))
        if lq == 3:
            while pending_opj:
                pump_outproj(*pending_opj.pop(0), all_now=True)
    B.close()
    s.barrier()

    NG2 = NGRP // 2
    C = contextlib.ExitStack()
    sbC = lambda *a: C.enter_context(nc.sbuf_tensor(*a))
    psC = lambda *a: C.enter_context(nc.psum_tensor(*a))
    perm = sbC("perm", [128, 8 * 64 * NG2], BF16)
    eo = [sbC(f"eo{hh}", [128, 8 * NG2 * 64], BF16) for hh in range(2)]
    w1st = [sbC(f"w1st{i}", [128, 256], F32) for i in range(4)]
    w1b = [sbC(f"w1b{i}", [128, 256], BF16) for i in range(4)]
    w2st = [sbC(f"w2st{i}", [128, DM], F32) for i in range(2)]
    w2b = [sbC(f"w2b{i}", [128, DM], BF16) for i in range(4)]
    inner = sbC("inner", [128, NG2 * 16], BF16)
    eot = [sbC(f"eot{i}", [128, DM], BF16) for i in range(2)]
    ffn_sb = [sbC(f"ffnsb{i}", [128, DM], F32) for i in range(2)]
    pg_ps = [psC(f"pg_ps{i}", [128, 512], F32) for i in range(2)]
    in_ps = [psC("in_ps0", [128, NG2], F32)] * 2
    eo_ps = [psC(f"eo_ps{i}", [128, 8 * NG2], F32) for i in range(2)]
    eot_ps = psC("eot_ps", [128, DM], BF16)
    res_ps = psC("res_ps", [128, DM], F32)

    w1_last = {}
    w2_last = {}
    w2c_last = {}
    eot_guard = None
    res_evac = None
    ffn_dma = {}
    perm_guard = []
    eo_guard = []
    for gb in range(2):
        eo_done = []
        for hh in range(2):
            gev = {}
            for gg in range(NG2):
                g = gb * NG2 + gg
                pb = gg % 2
                mm = None
                for dmc in range(8):
                    mm = s.op("tensor", lambda e, g=g, dmc=dmc, pb=pb, hh=hh: e.matmul(
                        pg_ps[pb][:, 64 * dmc:64 * (dmc + 1)],
                        xot[g // 4][32 * (g % 4):32 * (g % 4) + 32, 128 * dmc:128 * (dmc + 1)],
                        maskT[32 * (g % 4):32 * (g % 4) + 32,
                              128 * (g // 4) + 64 * hh:128 * (g // 4) + 64 * hh + 64],
                        start=True, stop=True, tile_position=(32 * (g % 4), 0)),
                        deps=([gev.get(pb)] + perm_guard) if dmc == 0 else [])
                gev[pb] = s.op("scalar", lambda e, gg=gg, pb=pb: e.copy(
                    perm[:].rearrange("p (d n g_) -> p d n g_", n=64, g_=NG2)[:, :, :, gg],
                    pg_ps[pb][:].rearrange("p (d n) -> p d n", n=64)), deps=[mm])
            perm_guard = []
            gather_done = list(gev.values())
            if DEBUG_L2 and gb == 0 and hh == 0:
                gather_done.append(s.op("sync", lambda e: e.dma_start(out=o_perm, in_=perm[:]),
                                        deps=gather_done, dma="dbg"))
            relu_ops = []
            eoev = {}
            for j in range(16):
                jg = 16 * hh + j
                mm_stack = []
                for i in range(4):
                    esp = 4 * j + i
                    es = 64 * hh + esp
                    st, bf = w1st[esp % 4], w1b[esp % 4]
                    dwh = [s.op("sync", lambda e, es=es, st=st, q=q: e.dma_start(
                        out=st[:].rearrange("p (d f) -> p d f", f=32)[:, 4 * q:4 * q + 4, :],
                        in_=w1_d[es:es + 1].rearrange("o (d p) f -> p (o d) f", p=128)[:, 4 * q:4 * q + 4, :]),
                        deps=[w1_last.get(esp % 4)], dma="w1s%d%s" % (esp % 4, "ab"[q]))
                        for q in range(2)]
                    cw_ = s.op("vector", lambda e, st=st, bf=bf: e.tensor_copy(bf[:], st[:]), deps=dwh)
                    for dmc in range(8):
                        mm = s.op("tensor", lambda e, esp=esp, i=i, dmc=dmc, bf=bf, j=j: e.matmul(
                            in_ps[j % 2][32 * i:32 * i + 32, :],
                            bf[:, 32 * dmc:32 * (dmc + 1)],
                            perm[:, 2048 * dmc + 32 * esp:2048 * dmc + 32 * esp + NG2],
                            start=(dmc == 0), stop=(dmc == 7), tile_position=(0, 32 * i)),
                            deps=[cw_] + (gather_done if esp == 0 else [])
                                 + ([relu_ops[j - 1]] if (dmc == 0 and i == 0 and j >= 1) else []))
                        mm_stack.append(mm)
                        w1_last[esp % 4] = mm
                relu_ops.append(s.op("scalar", lambda e, j=j, jg=jg: e.activation(
                    inner[:, NG2 * j:NG2 * (j + 1)], in_ps[j % 2][:], AF.Relu, bias=b1pk[:, jg:jg + 1]),
                    deps=mm_stack + [db1]))
                st2, bf2 = w2st[j % 2], w2b[j % 4]
                dw2h = [s.op("sync", lambda e, jg=jg, st=st2, q=q: e.dma_start(
                    out=st[:, 512 * q:512 * q + 512],
                    in_=w2T_d[4 * jg:4 * jg + 4].rearrange("e f m -> (e f) m")[:, 512 * q:512 * q + 512]),
                    deps=[w2_last.get(j % 4), w2c_last.get(j % 2)], dma="w2s%d%s" % (j % 2, "ab"[q]))
                    for q in range(2)]
                cw2 = s.op("vector", lambda e, st=st2, bf=bf2: e.tensor_copy(bf[:], st[:]), deps=dw2h)
                w2c_last[j % 2] = cw2
                for i in range(4):
                    esp = 4 * j + i
                    pb = esp % 2
                    mm = None
                    for dmc in range(8):
                        mm = s.op("tensor", lambda e, i=i, dmc=dmc, bf=bf2, j=j, pb=pb: e.matmul(
                            eo_ps[pb][:, NG2 * dmc:NG2 * (dmc + 1)],
                            bf[32 * i:32 * i + 32, 128 * dmc:128 * (dmc + 1)],
                            inner[32 * i:32 * i + 32, NG2 * j:NG2 * (j + 1)],
                            start=True, stop=True, tile_position=(32 * i, 0)),
                            deps=([cw2, relu_ops[j], eoev.get(pb)] + eo_guard) if dmc == 0 else [])
                        w2_last[j % 4] = mm
                    eoev[pb] = s.op("vector", lambda e, esp=esp, pb=pb, hh=hh: e.tensor_copy(
                        eo[hh][:].rearrange("p (d g_ n) -> p d g_ n", g_=NG2, n=64)[:, :, :, esp],
                        eo_ps[pb][:].rearrange("p (d g_) -> p d g_", g_=NG2)),
                        deps=[mm])
                eo_guard = []
            eo_done += list(eoev.values())
            perm_guard += [w1_last[k] for k in w1_last]
        for gg in range(NG2):
            g = gb * NG2 + gg
            ib = gg % 2
            trs = []
            for hh in range(2):
                for dmc in range(8):
                    trs.append(s.op("tensor", lambda e, gg=gg, hh=hh, dmc=dmc: e.transpose(
                        eot_ps[64 * hh:64 * hh + 64, 128 * dmc:128 * (dmc + 1)],
                        eo[hh][:, 2048 * dmc + 64 * gg:2048 * dmc + 64 * gg + 64], ident[:],
                        tile_position=(0, 64 * hh)),
                        deps=(eo_done + [eot_guard]) if (hh == 0 and dmc == 0 and gg < 2) else
                             ([eot_guard] if (hh == 0 and dmc == 0) else [])))
            tev = s.op("scalar", lambda e, ib=ib: e.copy(eot[ib][:], eot_ps[:]), deps=[trs[-1]])
            eot_guard = tev
            mm = None
            first = (g % 4 == 0)
            for nh in range(2):
                mm = s.op("tensor", lambda e, g=g, nh=nh, ib=ib: e.matmul(
                    res_ps[32 * (g % 4):32 * (g % 4) + 32, 512 * nh:512 * nh + 512],
                    mask_all[:, GRP * g:GRP * (g + 1)],
                    eot[ib][:, 512 * nh:512 * nh + 512],
                    start=True, stop=False, tile_position=(0, 32 * (g % 4))),
                    deps=[tev] + ([res_evac] if (first and nh == 0) else []))
                mm = s.op("tensor", lambda e, g=g, nh=nh: e.matmul(
                    res_ps[32 * (g % 4):32 * (g % 4) + 32, 512 * nh:512 * nh + 512],
                    mask_all[:, GRP * g:GRP * (g + 1)],
                    b2b[:, 512 * nh:512 * nh + 512],
                    start=False, stop=True, tile_position=(0, 32 * (g % 4))),
                    deps=[cb2] if (g < 2 and nh == 0) else [])
            if g % 4 == 3:
                sg = g // 4
                rev = s.op("vector", lambda e, sg=sg: e.tensor_copy(ffn_sb[sg % 2][:], res_ps[:]),
                           deps=[mm, ffn_dma.get(sg % 2)])
                res_evac = rev
                ffn_dma[sg % 2] = s.op("sync", lambda e, sg=sg: e.dma_start(
                    out=o_ffn[128 * sg:128 * (sg + 1), :], in_=ffn_sb[sg % 2][:]), deps=[rev], dma="offn")
        eo_guard = [trs[-1]]
    C.close()
    stk = s.build()
    return nc, stk



def build_L3():
    nc = bass.Bass("TRN2", target_bir_lowering=False)
    x_own = nc.dram_tensor("x_own", [LH, DM], F32, kind="ExternalInput").ap()
    attn_a = nc.dram_tensor("attn_a", [LH, DM], BF16, kind="ExternalInput").ap()
    attn_b = nc.dram_tensor("attn_b", [LH, DM], BF16, kind="ExternalInput").ap()
    ffn = nc.dram_tensor("ffn", [LH, DM], F32, kind="ExternalInput").ap()
    o = nc.dram_tensor("o", [LH, DM], F32, kind="ExternalOutput").ap()

    s = Sched(nc)
    sb = nc.alloc_sbuf_tensor
    ta = [sb(f"ta{i}", [128, DM], F32) for i in range(2)]
    tb = [sb(f"tb{i}", [128, DM], BF16) for i in range(2)]
    tc_ = [sb(f"tc{i}", [128, DM], BF16) for i in range(2)]
    td = [sb(f"td{i}", [128, DM], F32) for i in range(2)]
    guard = {}
    for c in range(LH // 128):
        i = c % 2
        d1 = s.op("sync", lambda e, i=i, c=c: e.dma_start(
            out=ta[i][:], in_=x_own[128 * c:128 * (c + 1), :]), deps=guard.get(i, []), dma="la")
        d2 = s.op("sync", lambda e, i=i, c=c: e.dma_start(
            out=tb[i][:], in_=attn_a[128 * c:128 * (c + 1), :]), deps=guard.get(i, []), dma="lb")
        d3 = s.op("sync", lambda e, i=i, c=c: e.dma_start(
            out=tc_[i][:], in_=attn_b[128 * c:128 * (c + 1), :]), deps=guard.get(i, []), dma="lc")
        d4 = s.op("sync", lambda e, i=i, c=c: e.dma_start(
            out=td[i][:], in_=ffn[128 * c:128 * (c + 1), :]), deps=guard.get(i, []), dma="ld")
        a1 = s.op("vector", lambda e, i=i: e.tensor_add(ta[i][:], ta[i][:], tb[i][:]), deps=[d1, d2])
        a2 = s.op("vector", lambda e, i=i: e.tensor_add(ta[i][:], ta[i][:], tc_[i][:]), deps=[d3])
        a3 = s.op("vector", lambda e, i=i: e.tensor_add(ta[i][:], ta[i][:], td[i][:]), deps=[d4])
        dd = s.op("sync", lambda e, i=i, c=c: e.dma_start(
            out=o[128 * c:128 * (c + 1), :], in_=ta[i][:]), deps=[a3], dma="out")
        guard[i] = [dd, a3]
    stk = s.build()
    return nc, stk



_CACHE = {}


def _bfview(a):
    return a if a.dtype == ml_dtypes.bfloat16 else a.view(ml_dtypes.bfloat16)


def _get(name, builder):
    if name not in _CACHE:
        _CACHE[name] = builder()
    return _CACHE[name]


def kernel(L0_attn_hidden_states, L0_attn_in_proj_w, L0_attn_conv_w, L0_attn_conv_b,
           L0_attn_x_proj_w, L0_attn_dt_proj_w, L0_attn_A_log, L0_attn_D, L0_attn_out_proj_w,
           L0_ffn_Wc, L0_ffn_bc, L0_ffn_W1, L0_ffn_b1, L0_ffn_W2, L0_ffn_b2):
    f32 = np.float32
    bf16 = ml_dtypes.bfloat16
    cont = np.ascontiguousarray
    x = np.asarray(L0_attn_hidden_states, f32)
    W_in = np.asarray(L0_attn_in_proj_w, f32)
    conv_w = np.asarray(L0_attn_conv_w, f32)
    conv_b = np.asarray(L0_attn_conv_b, f32)
    xp_w = np.asarray(L0_attn_x_proj_w, f32)
    dtp_w = np.asarray(L0_attn_dt_proj_w, f32)
    A_log = np.asarray(L0_attn_A_log, f32)
    Dv = np.asarray(L0_attn_D, f32)
    W_out = np.asarray(L0_attn_out_proj_w, f32)
    Wc = np.asarray(L0_ffn_Wc, f32).reshape(DM, NES)
    bcv = np.asarray(L0_ffn_bc, f32).reshape(NES, 1)
    W1 = np.asarray(L0_ffn_W1, f32).reshape(NES, DM, 32)
    b1 = np.asarray(L0_ffn_b1, f32).reshape(NES, 32)
    W2 = np.asarray(L0_ffn_W2, f32).reshape(NES, DM, 32)
    b2 = np.asarray(L0_ffn_b2, f32).reshape(NES, DM)

    identb = np.eye(128, dtype=bf16)
    tie = np.linspace(0.0, 1e-6, GRP, dtype=f32)
    tie_full = cont(np.broadcast_to(np.tile(tie, NGRP)[None, :], (128, LH)))
    b1p = cont(b1.reshape(32, 128).T)
    w2T = cont(W2.transpose(0, 2, 1))
    W_inT_f = cont(W_in.T)
    xpT_f = cont(xp_w.T)
    dtpT_f = cont(dtp_w.T)
    WoT_f = cont(W_out.T)

    in1, in2_base, in3 = [], [], []
    for c in range(8):
        b, h = c // 2, c % 2
        sl = slice(DH * h, DH * h + DH)
        w_inT_c = cont(np.concatenate([W_inT_f[:, DH * h:DH * h + DH],
                                       W_inT_f[:, 2048 + DH * h:2048 + DH * h + DH]], axis=1))
        in1.append(dict(
            x_tm=cont(x[b]), w_inT=w_inT_c,
            convw=cont(conv_w[sl]), convb=cont(conv_b[sl].reshape(8, 128).T),
            xp_T=cont(xpT_f[sl]), identb=identb))
        in2_base.append(dict(
            dtpT=cont(dtpT_f[:, sl]), alog=cont(A_log[sl]),
            dvec=cont(Dv[sl].reshape(8, 128).T), woutT=cont(WoT_f[sl]),
            x_own=cont(x[b, LH * h:LH * h + LH]),
            wc=Wc, bc=bcv, tie=tie_full, w1=W1, b1p=b1p, w2T=w2T, b2=b2,
            identb=identb))

    nc1, _ = _get("L1", build_L1)
    r1 = run_bass_kernel_spmd(nc1, in1, core_ids=list(range(8)))

    in2 = []
    for c in range(8):
        pc = c ^ 1
        d = dict(in2_base[c])
        d["xbf"] = np.asarray(r1.results[c]["o_xbf"]).view(bf16) \
            if np.asarray(r1.results[c]["o_xbf"]).dtype != bf16 else np.asarray(r1.results[c]["o_xbf"])
        d["szbf"] = np.asarray(r1.results[c]["o_szbf"]).view(bf16) \
            if np.asarray(r1.results[c]["o_szbf"]).dtype != bf16 else np.asarray(r1.results[c]["o_szbf"])
        d["xdbl0"] = np.asarray(r1.results[c]["o_xdbl"])
        d["xdbl1"] = np.asarray(r1.results[pc]["o_xdbl"])
        in2.append(d)

    nc2, _ = _get("L2", build_L2)
    r2 = run_bass_kernel_spmd(nc2, in2, core_ids=list(range(8)))

    in3 = []
    for c in range(8):
        b, h = c // 2, c % 2
        pc = c ^ 1
        in3.append(dict(
            x_own=cont(x[b, LH * h:LH * h + LH]),
            attn_a=cont(_bfview(np.asarray(r2.results[c]["o_attn"]))[LH * h:LH * h + LH]),
            attn_b=cont(_bfview(np.asarray(r2.results[pc]["o_attn"]))[LH * h:LH * h + LH]),
            ffn=np.asarray(r2.results[c]["o_ffn"])))
    nc3, _ = _get("L3", build_L3)
    r3 = run_bass_kernel_spmd(nc3, in3, core_ids=list(range(8)))

    out = np.empty((4, L, DM), f32)
    for c in range(8):
        b, h = c // 2, c % 2
        out[b, LH * h:LH * h + LH] = np.asarray(r3.results[c]["o"])
    return out


# version 13
# speedup vs baseline: 1.0815x; 1.0399x over previous
import contextlib
import numpy as np
import ml_dtypes
import concourse.bass as bass
import concourse.mybir as mybir
from concourse.bass_utils import run_bass_kernel_spmd

F32 = mybir.dt.float32
BF16 = mybir.dt.bfloat16
AF = mybir.ActivationFunctionType
ALU = mybir.AluOpType
AX = mybir.AxisListType.X

ENGINES = ("sync", "tensor", "vector", "scalar", "gpsimd")

DM = 1024
DH = 1024
L = 4096
LH = 2048
NST = 16
DTR = 64
NES = 128
GRP = 32
NGRP = LH // GRP
LQ = 1024
DEBUG_L2 = False


class Sched:

    paranoid = False

    def __init__(self, nc):
        self.nc = nc
        self.ops = []
        self.pending = {e: [] for e in ENGINES}
        self.last_in_stream = {}

    def op(self, eng, fn, deps=(), hard=(), dma=False):
        deps = [d for d in deps if d is not None] + self.pending[eng]
        if self.paranoid:
            for j in range(len(self.ops) - 1, -1, -1):
                if self.ops[j]["eng"] == eng and self.ops[j]["stream"] is None:
                    deps.append(j)
                    break
        stream = None
        if dma:
            stream = dma if isinstance(dma, str) else "dma"
            prev = self.last_in_stream.get(stream)
            if prev is not None:
                deps.append(prev)
        i = len(self.ops)
        self.ops.append(dict(eng=eng, fn=fn, stream=stream, mark=bool(dma),
                             deps=tuple(deps),
                             hard=tuple(d for d in hard if d is not None)))
        if stream is not None:
            self.last_in_stream[stream] = i
        return i

    def barrier(self):
        last = {}
        for i, o in enumerate(self.ops):
            last[o["eng"] if o["stream"] is None else o["stream"]] = i
        deps = list(last.values())
        for e in ENGINES:
            self.pending[e] = list(deps)

    def build(self):
        nc = self.nc
        for o in self.ops:
            for d in o["deps"]:
                dop = self.ops[d]
                if dop["stream"] is not None or dop["eng"] != o["eng"] or self.paranoid:
                    dop["mark"] = True
            for d in o["hard"]:
                self.ops[d]["mark"] = True
        stk = contextlib.ExitStack()
        keys = list(ENGINES) + sorted({o["stream"] for o in self.ops if o["stream"]})
        sems = {k: stk.enter_context(nc.semaphore("sem_" + k)) for k in keys}
        val = {k: 0 for k in keys}
        for o in self.ops:
            k = o["stream"] or o["eng"]
            if o["mark"]:
                val[k] += 16 if o["stream"] else 1
            o["val"] = val[k]
            o["key"] = k
        final = dict(val)

        with nc.Block() as block:
            for eng_name in ENGINES:
                def mk(e):
                    def body(engine):
                        waited = {}
                        for o in self.ops:
                            if o["eng"] != e:
                                continue
                            for d in tuple(o["deps"]) + tuple(o["hard"]):
                                dop = self.ops[d]
                                tk, tv = dop["key"], dop["val"]
                                if (dop["stream"] is None and dop["eng"] == e
                                        and d not in o["hard"] and not self.paranoid):
                                    continue
                                if waited.get(tk, 0) < tv:
                                    engine.wait_ge(sems[tk], tv)
                                    waited[tk] = tv
                            inst = o["fn"](engine)
                            if o["mark"]:
                                inst.then_inc(sems[o["key"]], 16 if o["stream"] else 1)
                        if e == "sync":
                            for tk in keys:
                                if tk != "sync" and final[tk] > waited.get(tk, 0):
                                    engine.wait_ge(sems[tk], final[tk])
                    return body
                getattr(block, eng_name)(mk(eng_name))
        return stk


def _ln_chunk(s, xb, xout, stat, sqs, dep_load, prev, cvals, cdep, xout_guard=None):
    m2_prev, sqs_prev = prev
    r1 = s.op("vector", lambda e: e.reduce_sum(stat[:, 0:1], xb[:], AX), deps=[dep_load])
    r2 = s.op("scalar", lambda e: e.activation(sqs[:], xb[:], AF.Square, accum_out=stat[:, 1:2]),
              deps=[dep_load, m2_prev, sqs_prev])
    m1 = s.op("vector", lambda e: e.tensor_scalar(stat[:, 2:3], stat[:, 0:1], 1.0 / DM, None, ALU.mult),
              deps=[r2], hard=[r1])
    m2 = s.op("vector", lambda e: e.tensor_scalar(stat[:, 3:4], stat[:, 1:2], 1.0 / DM, None, ALU.mult))
    m3 = s.op("vector", lambda e: e.tensor_tensor(stat[:, 4:5], stat[:, 2:3], stat[:, 2:3], ALU.mult),
              hard=[m1])
    m4 = s.op("vector", lambda e: e.tensor_tensor(stat[:, 5:6], stat[:, 3:4], stat[:, 4:5], ALU.subtract),
              hard=[m2, m3])
    v1 = s.op("scalar", lambda e: e.activation(stat[:, 6:7], stat[:, 5:6], AF.Ln, bias=cvals[:, 0:1]),
              deps=[m4, cdep])
    v2 = s.op("scalar", lambda e: e.activation(stat[:, 7:8], stat[:, 6:7], AF.Exp, scale=cvals[:, 1:2]),
              hard=[v1])
    xn = s.op("vector", lambda e: e.tensor_scalar(
        xout[:], xb[:], stat[:, 2:3], stat[:, 7:8], ALU.subtract, ALU.mult),
        deps=[v2, xout_guard], hard=[m1])
    return xn, r2, m2



def build_L1():
    nc = bass.Bass("TRN2", target_bir_lowering=False)
    x_tm = nc.dram_tensor("x_tm", [L, DM], F32, kind="ExternalInput").ap()
    w_inT = nc.dram_tensor("w_inT", [DM, 2 * DH], F32, kind="ExternalInput").ap()
    convw = nc.dram_tensor("convw", [DH, 4], F32, kind="ExternalInput").ap()
    convb = nc.dram_tensor("convb", [128, 8], F32, kind="ExternalInput").ap()
    xp_T = nc.dram_tensor("xp_T", [DH, 96], F32, kind="ExternalInput").ap()
    identb_d = nc.dram_tensor("identb", [128, 128], BF16, kind="ExternalInput").ap()

    o_xbf = nc.dram_tensor("o_xbf", [DH, L], BF16, kind="ExternalOutput").ap()
    o_szbf = nc.dram_tensor("o_szbf", [DH, L], BF16, kind="ExternalOutput").ap()
    o_xdbl = nc.dram_tensor("o_xdbl", [96, L], F32, kind="ExternalOutput").ap()

    s = Sched(nc)
    sb = nc.alloc_sbuf_tensor
    ps = nc.alloc_psum_tensor

    ident = sb("ident", [128, 128], BF16)
    xn_fm = [sb(f"xn_fm{k}", [128, L], BF16) for k in range(8)]
    w_stage = [sb(f"wst{i}", [128, DH], F32) for i in range(2)]
    w_bf = [sb(f"wbf{k}", [128, 2 * DH], BF16) for k in range(8)]
    cw = sb("cw", [128, 32], F32)
    cb = sb("cb", [128, 8], F32)
    xpt = sb("xpt", [128, 8 * 96], F32)
    xptb = sb("xptb", [128, 8 * 96], BF16)
    xch = [sb(f"xch{i}", [128, DM], F32) for i in range(2)]
    stat = [sb(f"stat{i}", [128, 8], F32) for i in range(2)]
    sqs = [sb(f"sqs{i}", [128, DM], F32) for i in range(2)]
    cvals = sb("cvals", [128, 3], F32)
    xnc = [sb(f"xnc{i}", [128, DM], BF16) for i in range(2)]
    xc = [sb(f"xc{r}", [128, 3 + LQ], F32) for r in range(8)]
    cacc = sb("cacc", [128, LQ], F32)
    cacc2 = [sb(f"cacc2{i}", [128, LQ], F32) for i in range(2)]
    xbf_t = [sb(f"xbf{r}", [128, LQ], BF16) for r in range(8)]
    szbf_t = [sb(f"szbf{i}", [128, LQ], BF16) for i in range(2)]
    xdbl_sb = [sb(f"xdbl_sb{i}", [96, LQ], F32) for i in range(2)]

    tpp = [ps(f"tpp{i}", [128, DM], BF16) for i in range(2)]
    ipp = [ps(f"ipp{i}", [128, 512], F32) for i in range(4)]
    xpp = ps("xpp", [96, LQ], F32)

    d_ident = s.op("sync", lambda e: e.dma_start(out=ident[:], in_=identb_d), dma="const")
    d_cw = s.op("sync", lambda e: e.dma_start(
        out=cw[:].rearrange("p (d k) -> p d k", k=4),
        in_=convw.rearrange("(d p) k -> p d k", p=128)), dma="const")
    d_cb = s.op("sync", lambda e: e.dma_start(out=cb[:], in_=convb), dma="const")
    d_xpt = s.op("sync", lambda e: e.dma_start(
        out=xpt[:].rearrange("p (d n) -> p d n", n=96),
        in_=xp_T.rearrange("(d p) n -> p d n", p=128)), dma="const")
    c_xpt = s.op("vector", lambda e: e.tensor_copy(xptb[:], xpt[:]), deps=[d_xpt])
    cv1 = s.op("vector", lambda e: e.memset(cvals[:, 0:1], 1e-5))
    cv2 = s.op("vector", lambda e: e.memset(cvals[:, 1:2], -0.5))
    cdep = s.op("vector", lambda e: e.memset(cvals[:, 2:3], 1.0))

    wcasts = []
    for kk in range(16):
        k, half = kk // 2, kk % 2
        st = w_stage[kk % 2]
        d = s.op("sync", lambda e, st=st, k=k, half=half: e.dma_start(
            out=st[:], in_=w_inT[128 * k:128 * (k + 1), DH * half:DH * half + DH]),
            deps=[wcasts[kk - 2] if kk >= 2 else None], dma="win")
        wcasts.append(s.op("vector", lambda e, st=st, k=k, half=half: e.tensor_copy(
            w_bf[k][:, DH * half:DH * half + DH], st[:]), deps=[d]))

    ev_prev = {}
    ln_prev = {}
    reader_prev = {}
    tr_last = {}
    for c in range(L // 128):
        xb = xch[c % 2]
        d = s.op("sync", lambda e, xb=xb, c=c: e.dma_start(
            out=xb[:], in_=x_tm[128 * c:128 * (c + 1), :]),
            deps=reader_prev.get(c % 2, []), dma="xin")
        xn, r2, m2 = _ln_chunk(s, xb, xnc[c % 2], stat[c % 2], sqs[c % 2], d,
                               ln_prev.get(c % 2, (None, None)), cvals, cdep,
                               xout_guard=tr_last.get(c % 2))
        ln_prev[c % 2] = (m2, None)
        reader_prev[c % 2] = [xn, r2]
        for k in range(8):
            pb = (c * 8 + k) % 2
            t = s.op("tensor", lambda e, pb=pb, k=k, c=c: e.transpose(
                tpp[pb][:, 0:128], xnc[c % 2][:, 128 * k:128 * (k + 1)], ident[:]),
                deps=[xn, d_ident, ev_prev.get(pb)])
            tr_last[c % 2] = t
            ev_prev[pb] = s.op("vector", lambda e, pb=pb, k=k, c=c: e.tensor_copy(
                xn_fm[k][:, 128 * c:128 * (c + 1)], tpp[pb][:, 0:128]), deps=[t])
    ln_done = list(ev_prev.values())

    czero = [s.op("vector", lambda e, r=r: e.memset(xc[r][:, 0:3], 0.0)) for r in range(8)]

    xproj_prev = None
    psum_guard = {}
    xbf_guard = {}
    szdma_prev = {}
    xc_guard = {}
    sil_par = {}
    sils, dxs = {}, {}
    for lc in range(4):
        xproj_mms = []
        pend_xp = []

        def emit_xproj(dc):
            xms = []
            for nh in range(2):
                xm = s.op("tensor", lambda e, dc=dc, nh=nh: e.matmul(
                    xpp[0:96, 512 * nh:512 * nh + 512],
                    xptb[:, 96 * dc:96 * (dc + 1)],
                    xbf_t[dc][:, 512 * nh:512 * nh + 512],
                    start=(dc == 0), stop=(dc == 7)),
                    deps=[sils[dc], c_xpt] + ([xproj_prev] if (dc == 0 and nh == 0) else []))
                xms.append(xm)
                xproj_mms.append(xm)
            xbf_guard[dc] = [dxs[dc]] + xms
        for rc in range(16):
            pp = ipp[rc % 4]
            evs = []
            for nh in range(2):
                last_mm = None
                dep0 = [wcasts[15], psum_guard.get(rc % 4)] + (ln_done if lc == 0 else [])
                if nh == 1:
                    dep0 = [evs[0]]
                for k in range(8):
                    last_mm = s.op("tensor", lambda e, pp=pp, k=k, nh=nh, rc=rc, lc=lc: e.matmul(
                        pp[:],
                        w_bf[k][:, 128 * rc:128 * (rc + 1)],
                        xn_fm[k][:, LQ * lc + 512 * nh:LQ * lc + 512 * nh + 512],
                        start=(k == 0), stop=(k == 7)),
                        deps=(dep0 if k == 0 else []))
                if rc < 8:
                    ev = s.op("scalar", lambda e, pp=pp, rc=rc, nh=nh: e.copy(
                        xc[rc][:, 3 + 512 * nh:3 + 512 * nh + 512], pp[:]),
                        deps=[last_mm, xc_guard.get(rc)])
                else:
                    ev = s.op("scalar", lambda e, pp=pp, rc=rc, nh=nh: e.activation(
                        szbf_t[rc % 2][:, 512 * nh:512 * nh + 512], pp[:], AF.Silu),
                        deps=[last_mm, szdma_prev.get(rc % 2)])
                evs.append(ev)
            psum_guard[rc % 4] = evs[1]
            if rc < 8:
                dc = rc
                a0 = s.op("vector", lambda e, dc=dc: e.tensor_scalar(
                    cacc[:], xc[dc][:, 0:LQ], cw[:, 4 * dc:4 * dc + 1], None, ALU.mult),
                    deps=[evs[1], d_cw, czero[dc]])
                cc2 = cacc2[dc % 2]
                a1 = s.op("vector", lambda e, dc=dc, cc2=cc2: e.scalar_tensor_tensor(
                    cc2[:], xc[dc][:, 1:1 + LQ], cw[:, 4 * dc + 1:4 * dc + 2], cacc[:],
                    ALU.mult, ALU.add), deps=[sil_par.get(dc % 2)])
                a2 = s.op("vector", lambda e, dc=dc, cc2=cc2: e.scalar_tensor_tensor(
                    cacc[:], xc[dc][:, 2:2 + LQ], cw[:, 4 * dc + 2:4 * dc + 3], cc2[:],
                    ALU.mult, ALU.add))
                a3 = s.op("vector", lambda e, dc=dc, cc2=cc2: e.scalar_tensor_tensor(
                    cc2[:], xc[dc][:, 3:3 + LQ], cw[:, 4 * dc + 3:4 * dc + 4], cacc[:],
                    ALU.mult, ALU.add))
                ccp = s.op("vector", lambda e, dc=dc: e.tensor_copy(xc[dc][:, 0:3], xc[dc][:, LQ:LQ + 3]),
                           deps=[a3])
                xc_guard[dc] = ccp
                sil = s.op("scalar", lambda e, dc=dc, cc2=cc2: e.activation(
                    xbf_t[dc][:], cc2[:], AF.Silu, bias=cb[:, dc:dc + 1]),
                    deps=[a3, d_cb] + xbf_guard.get(dc, []))
                sil_par[dc % 2] = sil
                dx = s.op("sync", lambda e, dc=dc, lc=lc: e.dma_start(
                    out=o_xbf[128 * dc:128 * (dc + 1), LQ * lc:LQ * (lc + 1)], in_=xbf_t[dc][:]),
                    deps=[sil], dma="oxbf")
                sils[dc] = sil
                dxs[dc] = dx
                pend_xp.append(dc)
                if len(pend_xp) > 2:
                    emit_xproj(pend_xp.pop(0))
            else:
                zi = rc - 8
                dz = s.op("sync", lambda e, zi=zi, lc=lc: e.dma_start(
                    out=o_szbf[128 * zi:128 * (zi + 1), LQ * lc:LQ * (lc + 1)], in_=szbf_t[zi % 2][:]),
                    deps=[evs[1]], dma="oszbf")
                szdma_prev[zi % 2] = dz
        while pend_xp:
            emit_xproj(pend_xp.pop(0))
        xev = s.op("vector", lambda e, lc=lc: e.tensor_copy(
            xdbl_sb[lc % 2][0:96, :], xpp[0:96, :]),
            deps=[xproj_mms[-1], xproj_prev])
        xproj_prev = s.op("sync", lambda e, lc=lc: e.dma_start(
            out=o_xdbl[:, LQ * lc:LQ * (lc + 1)], in_=xdbl_sb[lc % 2][0:96, :]),
            deps=[xev], dma="oxdbl")
    stk = s.build()
    return nc, stk



def build_L2():
    nc = bass.Bass("TRN2", target_bir_lowering=False)
    xbf_d = nc.dram_tensor("xbf", [DH, L], BF16, kind="ExternalInput").ap()
    szbf_d = nc.dram_tensor("szbf", [DH, L], BF16, kind="ExternalInput").ap()
    xdbl0_d = nc.dram_tensor("xdbl0", [96, L], F32, kind="ExternalInput").ap()
    xdbl1_d = nc.dram_tensor("xdbl1", [96, L], F32, kind="ExternalInput").ap()
    dtpT_d = nc.dram_tensor("dtpT", [DTR, DH], F32, kind="ExternalInput").ap()
    alog_d = nc.dram_tensor("alog", [DH, NST], F32, kind="ExternalInput").ap()
    dvec_d = nc.dram_tensor("dvec", [128, 8], F32, kind="ExternalInput").ap()
    woutT_d = nc.dram_tensor("woutT", [DH, DM], F32, kind="ExternalInput").ap()
    x_own_d = nc.dram_tensor("x_own", [LH, DM], F32, kind="ExternalInput").ap()
    wc_d = nc.dram_tensor("wc", [DM, NES], F32, kind="ExternalInput").ap()
    bc_d = nc.dram_tensor("bc", [NES, 1], F32, kind="ExternalInput").ap()
    tie_d = nc.dram_tensor("tie", [128, LH], F32, kind="ExternalInput").ap()
    w1_d = nc.dram_tensor("w1", [NES, DM, 32], F32, kind="ExternalInput").ap()
    b1p_d = nc.dram_tensor("b1p", [128, 32], F32, kind="ExternalInput").ap()
    w2T_d = nc.dram_tensor("w2T", [NES, 32, DM], F32, kind="ExternalInput").ap()
    b2_d = nc.dram_tensor("b2", [NES, DM], F32, kind="ExternalInput").ap()
    identb_d = nc.dram_tensor("identb", [128, 128], BF16, kind="ExternalInput").ap()

    o_attn = nc.dram_tensor("o_attn", [L, DM], BF16, kind="ExternalOutput").ap()
    o_ffn = nc.dram_tensor("o_ffn", [LH, DM], F32, kind="ExternalOutput").ap()
    bc_dram = nc.dram_tensor("bc_dram", [32, L], BF16, kind="Internal").ap()
    if DEBUG_L2:
        o_mask = nc.dram_tensor("o_mask", [128, LH], BF16, kind="ExternalOutput").ap()
        o_logit = nc.dram_tensor("o_logit", [128, LH], F32, kind="ExternalOutput").ap()
        o_xot = nc.dram_tensor("o_xot", [LH, DM], BF16, kind="ExternalOutput").ap()
        o_perm = nc.dram_tensor("o_perm", [128, 8 * 64 * 32], BF16, kind="ExternalOutput").ap()

    s = Sched(nc)
    sb = nc.alloc_sbuf_tensor

    ident = sb("ident", [128, 128], BF16)
    xdbl = sb("xdbl", [96, L], F32)
    xot = [sb(f"xot{c}", [128, DM], BF16) for c in range(16)]
    mask_all = sb("mask_all", [128, LH], BF16)
    maskT = sb("maskT", [128, (NGRP // 4) * 128], BF16)
    amat = sb("amat", [128, 8 * NST], F32)
    dvec = sb("dvec_sb", [128, 8], F32)
    dtpb = sb("dtpb", [DTR, DH], F32)
    carry = [sb(f"carry{r}", [128, NST], BF16) for r in range(8)]
    b2b = sb("b2b", [128, DM], BF16)
    b1pk = sb("b1pk", [128, 32], F32)
    cvals = sb("cvals", [128, 3], F32)

    d_ident = s.op("sync", lambda e: e.dma_start(out=ident[:], in_=identb_d), dma="const")
    cv1 = s.op("vector", lambda e: e.memset(cvals[:, 0:1], 1e-5))
    cv2 = s.op("vector", lambda e: e.memset(cvals[:, 1:2], -0.5))
    cdep = s.op("vector", lambda e: e.memset(cvals[:, 2:3], 1.0))
    da = s.op("sync", lambda e: e.dma_start(
        out=amat[:].rearrange("p (d n) -> p d n", n=NST),
        in_=alog_d.rearrange("(d p) n -> p d n", p=128)), dma="const")
    ae = s.op("scalar", lambda e: e.activation(amat[:], amat[:], AF.Exp), deps=[da])
    an = s.op("vector", lambda e: e.tensor_scalar(amat[:], amat[:], -1.0, None, ALU.mult), deps=[ae])
    dd = s.op("sync", lambda e: e.dma_start(out=dvec[:], in_=dvec_d), dma="const")
    ddtp = s.op("sync", lambda e: e.dma_start(out=dtpb[0:DTR, :], in_=dtpT_d), dma="const")
    db1 = s.op("sync", lambda e: e.dma_start(out=b1pk[:], in_=b1p_d), dma="const")

    A = contextlib.ExitStack()
    sbA = lambda *a: A.enter_context(nc.sbuf_tensor(*a))
    psA = lambda *a: A.enter_context(nc.psum_tensor(*a))
    xdbl_p1 = sbA("xdbl_p1", [96, L], F32)
    stat = [sbA(f"stat{i}", [128, 8], F32) for i in range(2)]
    sqs = [sbA(f"sqs{i}", [128, DM], F32) for i in range(2)]
    xch = [sbA(f"xch{i}", [128, DM], F32) for i in range(2)]
    b2stg = sbA("b2stg", [128, DM], F32)
    xdbl_bc = sbA("xdbl_bc", [32, L], BF16)
    xof_hi = [sbA(f"xofh{k}", [128, LH], BF16) for k in range(8)]
    xof_lo = [sbA(f"xofl{k}", [128, LH], BF16) for k in range(8)]
    logit = sbA("logit", [128, LH], F32)
    mxa = sbA("mxa", [128, NGRP], F32)
    ties = sbA("ties", [128, LH], F32)
    wcs = sbA("wcs", [128, 8 * NES], F32)
    wch = sbA("wch", [128, 8 * NES], BF16)
    wcl = sbA("wcl", [128, 8 * NES], BF16)
    wclf = sbA("wclf", [128, 8 * NES], F32)
    bcs = sbA("bcs", [128, 1], F32)
    xlo_c = [sbA(f"xlo{i}", [128, DM], BF16) for i in range(2)]
    tppA = [psA(f"tppA{i}", [128, DM], BF16) for i in range(2)]
    lgp = [psA(f"lgp{i}", [128, 512], F32) for i in range(2)]
    mtp = psA("mtp", [128, 128], BF16)

    db2 = s.op("sync", lambda e: e.dma_start(out=b2stg[:], in_=b2_d), dma="const")
    cb2 = s.op("vector", lambda e: e.tensor_copy(b2b[:], b2stg[:]), deps=[db2])
    d0 = s.op("sync", lambda e: e.dma_start(out=xdbl[0:96, :], in_=xdbl0_d), dma="xdl")
    d1 = s.op("sync", lambda e: e.dma_start(out=xdbl_p1[0:96, :], in_=xdbl1_d), dma="xdl")
    xsum = s.op("vector", lambda e: e.tensor_add(xdbl[0:96, :], xdbl[0:96, :], xdbl_p1[0:96, :]),
                deps=[d0, d1])
    xcast = s.op("vector", lambda e: e.tensor_copy(xdbl_bc[0:32, :], xdbl[64:96, :]), deps=[xsum])
    dbc = s.op("sync", lambda e: e.dma_start(out=bc_dram, in_=xdbl_bc[0:32, :]), deps=[xcast], dma="xdl")

    dwc = s.op("sync", lambda e: e.dma_start(
        out=wcs[:].rearrange("p (d n) -> p d n", n=NES),
        in_=wc_d.rearrange("(d p) n -> p d n", p=128)), dma="const")
    dbcb = s.op("sync", lambda e: e.dma_start(out=bcs[:], in_=bc_d), dma="const")
    dtie = s.op("sync", lambda e: e.dma_start(out=ties[:], in_=tie_d), dma="const")
    ch1 = s.op("vector", lambda e: e.tensor_copy(wch[:], wcs[:]), deps=[dwc])
    ch2 = s.op("vector", lambda e: e.tensor_sub(wclf[:], wcs[:], wch[:]))
    ch3 = s.op("vector", lambda e: e.tensor_copy(wcl[:], wclf[:]))

    ev_prev = {}
    ln_prev = {}
    reader_prev = {}
    t2_last = {}
    for c in range(16):
        xb = xch[c % 2]
        d = s.op("sync", lambda e, xb=xb, c=c: e.dma_start(
            out=xb[:], in_=x_own_d[128 * c:128 * (c + 1), :]),
            deps=reader_prev.get(c % 2, []), dma="xin")
        st_, sq_ = stat[c % 2], sqs[c % 2]
        xn, r2, m2 = _ln_chunk(s, xb, xot[c], st_, sq_, d, ln_prev.get(c % 2, (None, None)),
                               cvals, cdep)
        lo1 = s.op("vector", lambda e, xb=xb, st_=st_, sq_=sq_: e.tensor_scalar(
            sq_[:], xb[:], st_[:, 2:3], st_[:, 7:8], ALU.subtract, ALU.mult), hard=[xn], deps=[r2])
        lo2 = s.op("vector", lambda e, c=c, sq_=sq_: e.tensor_sub(xlo_c[c % 2][:], sq_[:], xot[c][:]),
                   deps=[t2_last.get(c % 2)])
        ln_prev[c % 2] = (m2, lo2)
        reader_prev[c % 2] = [lo2, r2]
        for k in range(8):
            pb = (c * 8 + k) % 2
            t1 = s.op("tensor", lambda e, pb=pb, k=k, c=c: e.transpose(
                tppA[pb][:, 0:128], xot[c][:, 128 * k:128 * (k + 1)], ident[:]),
                deps=[xn, d_ident, ev_prev.get(pb)])
            e1 = s.op("vector", lambda e, pb=pb, k=k, c=c: e.tensor_copy(
                xof_hi[k][:, 128 * c:128 * (c + 1)], tppA[pb][:, 0:128]), deps=[t1])
            t2 = s.op("tensor", lambda e, pb=pb, k=k, c=c: e.transpose(
                tppA[pb][:, 128:256], xlo_c[c % 2][:, 128 * k:128 * (k + 1)], ident[:]),
                deps=[lo2, e1])
            t2_last[c % 2] = t2
            ev_prev[pb] = s.op("vector", lambda e, pb=pb, k=k, c=c: e.tensor_copy(
                xof_lo[k][:, 128 * c:128 * (c + 1)], tppA[pb][:, 128:256]), deps=[t2])
    ln_done = list(ev_prev.values())

    lg_ev = []
    for q in range(LH // 512):
        pp = lgp[q % 2]
        mm = None
        for p3 in range(3):
            lhs, rhs = [(wch, xof_hi), (wch, xof_lo), (wcl, xof_hi)][p3]
            for k in range(8):
                mm = s.op("tensor", lambda e, pp=pp, k=k, q=q, lhs=lhs, rhs=rhs, p3=p3: e.matmul(
                    pp[:], lhs[:, NES * k:NES * (k + 1)], rhs[k][:, 512 * q:512 * q + 512],
                    start=(p3 == 0 and k == 0), stop=(p3 == 2 and k == 7)),
                    deps=(([ch1, ch3] + ln_done + [lg_ev[q - 2] if q >= 2 else None])
                          if (p3 == 0 and k == 0) else []))
        lg_ev.append(s.op("scalar", lambda e, pp=pp, q=q: e.activation(
            logit[:, 512 * q:512 * q + 512], pp[:], AF.Identity, bias=bcs[:]), deps=[mm, dbcb]))
    tie_add = s.op("vector", lambda e: e.tensor_add(logit[:], logit[:], ties[:]), deps=lg_ev + [dtie])
    red = s.op("vector", lambda e: e.tensor_reduce(
        mxa[:], logit[:].rearrange("p (g t) -> p g t", t=GRP), AX, ALU.max), deps=[tie_add])
    mk_ops = []
    for g in range(NGRP):
        mk_ops.append(s.op("vector", lambda e, g=g: e.tensor_scalar(
            mask_all[:, GRP * g:GRP * (g + 1)], logit[:, GRP * g:GRP * (g + 1)],
            mxa[:, g:g + 1], None, ALU.is_equal), hard=[red]))
    if DEBUG_L2:
        s.op("sync", lambda e: e.dma_start(out=o_logit, in_=logit[:]), deps=[red], dma="dbg")
    mt_prev = None
    for g in range(NGRP):
        t = s.op("tensor", lambda e, g=g: e.transpose(
            mtp[32 * (g % 4):32 * (g % 4) + 32, 0:128], mask_all[:, GRP * g:GRP * (g + 1)], ident[:],
            tile_position=(0, 32 * (g % 4))),
            deps=[mk_ops[g], mt_prev])
        mt_prev = s.op("vector", lambda e, g=g: e.tensor_copy(
            maskT[32 * (g % 4):32 * (g % 4) + 32, 128 * (g // 4):128 * (g // 4) + 128],
            mtp[32 * (g % 4):32 * (g % 4) + 32, 0:128]), deps=[t])

    if DEBUG_L2:
        s.op("sync", lambda e: e.dma_start(out=o_mask, in_=mask_all[:]), deps=mk_ops, dma="dbg")
        for c in range(16):
            s.op("sync", lambda e, c=c: e.dma_start(
                out=o_xot[128 * c:128 * (c + 1), :], in_=xot[c][:]), deps=[mt_prev], dma="dbg")
    A.close()
    s.barrier()

    B = contextlib.ExitStack()
    sbB = lambda *a: B.enter_context(nc.sbuf_tensor(*a))
    psB = lambda *a: B.enter_context(nc.psum_tensor(*a))
    bbc = sbB("bbc", [128, NST * LQ], BF16)
    cbc = sbB("cbc", [128, NST * LQ], BF16)
    h_rot = [sbB(f"hrot{i}", [128, LQ], BF16) for i in range(2)]
    out_bf = [[sbB(f"obf{p}_{r}", [128, LQ], BF16) for r in range(8)] for p in range(2)]
    woT = [sbB(f"wot{k}", [128, DM], BF16) for k in range(8)]
    wost = [sbB("wost0", [128, 256], F32)] * 2
    xbf_t = [sbB(f"xbft{i}", [128, LQ], BF16) for i in range(2)]
    szb_t = [sbB(f"szbt{i}", [128, LQ], BF16) for i in range(2)]
    delt = [sbB(f"delt{i}", [128, LQ], BF16) for i in range(2)]
    delx = [sbB(f"delx{i}", [128, LQ], BF16) for i in range(2)]
    ebuf = [sbB(f"ebuf{i}", [128, LQ], BF16) for i in range(2)]
    wbuf = sbB("wbuf", [128, LQ], BF16)
    pbuf = [sbB(f"pbuf{i}", [128, LQ], BF16) for i in range(2)]
    at_sb = [sbB("atsb0", [128, 512], BF16)] * 2
    dt_ps = psB("dt_ps", [128, LQ], F32)
    y_ps = psB("y_ps", [128, LQ], F32)
    at_ps = [psB(f"at_ps{i}", [128, 512], F32) for i in range(2)]

    wo_casts = []
    for kk in range(32):
        k, half = kk // 4, kk % 4
        st = wost[kk % 2]
        d = s.op("sync", lambda e, st=st, k=k, half=half: e.dma_start(
            out=st[:], in_=woutT_d[128 * k:128 * (k + 1), 256 * half:256 * half + 256]),
            deps=[wo_casts[kk - 1] if kk >= 1 else None], dma="wo")
        wo_casts.append(s.op("vector", lambda e, st=st, k=k, half=half: e.tensor_copy(
            woT[k][:, 256 * half:256 * half + 256], st[:]), deps=[d]))
    zinit = [s.op("vector", lambda e, r=r: e.memset(carry[r][:], 0.0)) for r in range(8)]

    bn_readers, cn_readers = {}, {}
    x_readers = {}
    chains = {}

    def emit_chain(g):
        lq, dc = divmod(g, 8)
        dci = g % 2
        dxt = s.op("sync", lambda e, dc=dc, lq=lq, dci=dci: e.dma_start(
            out=xbf_t[dci][:], in_=xbf_d[128 * dc:128 * (dc + 1), LQ * lq:LQ * (lq + 1)]),
            deps=x_readers.get(dci, []), dma="xl")
        dst = s.op("sync", lambda e, dc=dc, lq=lq, dci=dci: e.dma_start(
            out=szb_t[dci][:], in_=szbf_d[128 * dc:128 * (dc + 1), LQ * lq:LQ * (lq + 1)]),
            deps=x_readers.get(dci, []), dma="szl")
        dtm = None
        for nh in range(2):
            dtm = s.op("tensor", lambda e, dc=dc, lq=lq, nh=nh: e.matmul(
                dt_ps[:, 512 * nh:512 * nh + 512],
                dtpb[0:DTR, 128 * dc:128 * (dc + 1)],
                xdbl[0:DTR, LQ * lq + 512 * nh:LQ * lq + 512 * nh + 512],
                start=True, stop=True),
                deps=[xsum, ddtp, state["dtexp"]] if nh == 0 else [])
        ex = s.op("scalar", lambda e, dci=dci: e.activation(delx[dci][:], dt_ps[:], AF.Exp),
                  deps=[dtm] + x_readers.get(dci, []))
        state["dtexp"] = ex
        splus = s.op("scalar", lambda e, dci=dci: e.activation(
            delt[dci][:], delx[dci][:], AF.Ln, bias=cvals[:, 2:3]),
            deps=[ex, cdep] + delt_guard.get(dci, []))
        dxm = s.op("vector", lambda e, dci=dci: e.tensor_mul(delx[dci][:], delt[dci][:], xbf_t[dci][:]),
                   deps=[splus, dxt])
        return (dxt, dst, splus, dxm)

    state = {"dtexp": None}
    gate_prev = None
    pending_opj = []
    outproj_guard = {}
    opj_gen = [None]

    def _opj_ops(lq, g2dep):
        for tc in range(8):
            pb = tc % 2
            for nh in range(2):
                mm = None
                for dc in range(8):
                    mm = s.op("tensor", lambda e, dc=dc, nh=nh, tc=tc, pb=pb, lq=lq: e.matmul(
                        at_ps[pb][:],
                        out_bf[lq % 2][dc][:, 128 * tc:128 * (tc + 1)],
                        woT[dc][:, 512 * nh:512 * nh + 512],
                        start=(dc == 0), stop=(dc == 7)),
                        deps=([g2dep, wo_casts[31], atev_prev.get(pb)] if dc == 0 else []))
                ev = s.op("vector", lambda e, pb=pb, nh=nh: e.tensor_copy(
                    at_sb[pb][:], at_ps[pb][:]),
                    deps=[mm, atdma_prev.get("last")])
                atev_prev[pb] = ev
                for dc in range(8):
                    outproj_guard[(lq % 2, dc)] = mm
                atdma_prev["last"] = s.op("sync", lambda e, pb=pb, tc=tc, lq=lq, nh=nh: e.dma_start(
                    out=o_attn[LQ * lq + 128 * tc:LQ * lq + 128 * (tc + 1), 512 * nh:512 * nh + 512],
                    in_=at_sb[pb][:]),
                    deps=[atev_prev[pb]], dma="oattn")
            yield

    def pump_outproj(lq, g2dep, all_now=False):
        if opj_gen[0] is None:
            opj_gen[0] = _opj_ops(lq, g2dep)
        try:
            while True:
                next(opj_gen[0])
                if not all_now:
                    return
        except StopIteration:
            opj_gen[0] = None

    ebuf_guard = {}
    hrot_guard = {}
    pbuf_guard = {}
    delt_guard = {}
    carry_guard = {}
    outproj_last = None
    atev_prev = {}
    atdma_prev = {}
    it = 0
    for lq in range(4):
        bld_b, bld_c = {}, {}
        for n in range(NST):
            bld_b[n] = s.op("sync", lambda e, n=n, lq=lq: e.dma_start(
                out=bbc[:, LQ * n:LQ * (n + 1)],
                in_=bc_dram[n:n + 1, LQ * lq:LQ * (lq + 1)].broadcast_to([128, LQ])),
                deps=[dbc, bn_readers.get(n)], dma="bl%d" % (n % 4))
            bld_c[n] = s.op("sync", lambda e, n=n, lq=lq: e.dma_start(
                out=cbc[:, LQ * n:LQ * (n + 1)],
                in_=bc_dram[NST + n:NST + n + 1, LQ * lq:LQ * (lq + 1)].broadcast_to([128, LQ])),
                deps=[dbc, cn_readers.get(n)], dma="cl%d" % (n % 4))
        for dc in range(8):
            g = lq * 8 + dc
            dci = g % 2
            if g == 0:
                chains[0] = emit_chain(0)
            ch = chains.pop(g)
            dxt, dst, splus, dxm = ch
            pm = None
            for n in range(NST):
                ib = it % 2
                eo = s.op("scalar", lambda e, ib=ib, dci=dci, dc=dc, n=n: e.activation(
                    ebuf[ib][:], delt[dci][:], AF.Exp, scale=amat[:, NST * dc + n:NST * dc + n + 1]),
                    deps=[splus, an, ebuf_guard.get(ib)])
                wo = s.op("vector", lambda e, dci=dci, n=n: e.tensor_mul(
                    wbuf[:], delx[dci][:], bbc[:, LQ * n:LQ * (n + 1)]),
                    deps=[dxm, bld_b[n]])
                sc = s.op("vector", lambda e, ib=ib, n=n, dc=dc: e.tensor_tensor_scan(
                    h_rot[ib][:], ebuf[ib][:], wbuf[:], carry[dc][:, n:n + 1], ALU.mult, ALU.add),
                    deps=[eo, wo, hrot_guard.get(ib), carry_guard.get((dc, n))], hard=[zinit[dc]])
                ebuf_guard[ib] = sc
                cu = s.op("gpsimd", lambda e, dc=dc, n=n, ib=ib: e.tensor_copy(
                    carry[dc][:, n:n + 1], h_rot[ib][:, LQ - 1:LQ]), deps=[sc])
                carry_guard[(dc, n)] = cu
                po = s.op("vector", lambda e, n=n, ib=ib: e.tensor_mul(
                    pbuf[ib][:], h_rot[ib][:], cbc[:, LQ * n:LQ * (n + 1)]),
                    deps=[sc, bld_c[n], pbuf_guard.get(ib)])
                if dc == 7:
                    bn_readers[n] = wo
                    cn_readers[n] = po
                hrot_guard[ib] = cu
                for nh in range(2):
                    pm = s.op("tensor", lambda e, n=n, nh=nh, ib=ib: e.matmul(
                        y_ps[:, 512 * nh:512 * nh + 512], ident[:],
                        pbuf[ib][:, 512 * nh:512 * nh + 512],
                        start=(n == 0), stop=(n == NST - 1)),
                        deps=[po, d_ident] + ([gate_prev] if (n == 0 and nh == 0) else []))
                pbuf_guard[ib] = pm
                it += 1
                if n == 1 and g + 1 < 32:
                    chains[g + 1] = emit_chain(g + 1)
            g1 = s.op("vector", lambda e, dci=dci, dc=dc: e.scalar_tensor_tensor(
                wbuf[:], xbf_t[dci][:], dvec[:, dc:dc + 1], y_ps[:], ALU.mult, ALU.add),
                deps=[pm, dd])
            g2 = s.op("vector", lambda e, dci=dci, dc=dc, lq=lq: e.tensor_mul(
                out_bf[lq % 2][dc][:], wbuf[:], szb_t[dci][:]),
                deps=[g1, dst, outproj_guard.get((lq % 2, dc))])
            gate_prev = g1
            delt_guard[dci] = [g1]
            x_readers[dci] = [dxm, g1, g2]
            for _ in range(2):
                if pending_opj and pending_opj[0][0] < lq:
                    pump_outproj(*pending_opj[0])
                    if opj_gen[0] is None:
                        pending_opj.pop(0)
        pending_opj.append((lq, g2))
        if lq == 3:
            while pending_opj:
                pump_outproj(*pending_opj.pop(0), all_now=True)
    B.close()
    s.barrier()

    NG2 = NGRP // 2
    C = contextlib.ExitStack()
    sbC = lambda *a: C.enter_context(nc.sbuf_tensor(*a))
    psC = lambda *a: C.enter_context(nc.psum_tensor(*a))
    perm = sbC("perm", [128, 8 * 64 * NG2], BF16)
    eo = [sbC(f"eo{hh}", [128, 8 * NG2 * 64], BF16) for hh in range(2)]
    w1st = [sbC(f"w1st{i}", [128, 256], F32) for i in range(4)]
    w1b = [sbC(f"w1b{i}", [128, 256], BF16) for i in range(4)]
    w2st = [sbC(f"w2st{i}", [128, DM], F32) for i in range(2)]
    w2b = [sbC(f"w2b{i}", [128, DM], BF16) for i in range(4)]
    inner = sbC("inner", [128, NG2 * 16], BF16)
    eot = [sbC(f"eot{i}", [128, DM], BF16) for i in range(2)]
    ffn_sb = [sbC(f"ffnsb{i}", [128, DM], F32) for i in range(2)]
    pg_ps = [psC(f"pg_ps{i}", [128, 512], F32) for i in range(2)]
    in_ps0 = psC("in_ps0", [128, NG2], F32)
    in_view = lambda j: in_ps0 if j % 2 == 0 else pg_ps[0]
    eo_ps = [psC(f"eo_ps{i}", [128, 8 * NG2], F32) for i in range(2)]
    eot_ps = psC("eot_ps", [128, DM], BF16)
    res_ps = psC("res_ps", [128, DM], F32)

    w1_last = {}
    w2_last = {}
    w2c_last = {}
    eot_guard = None
    res_evac = None
    ffn_dma = {}
    perm_guard = []
    eo_guard = []
    for gb in range(2):
        eo_done = []
        for hh in range(2):
            gev = {}
            for gg in range(NG2):
                g = gb * NG2 + gg
                pb = gg % 2
                mm = None
                for dmc in range(8):
                    mm = s.op("tensor", lambda e, g=g, dmc=dmc, pb=pb, hh=hh: e.matmul(
                        pg_ps[pb][:, 64 * dmc:64 * (dmc + 1)],
                        xot[g // 4][32 * (g % 4):32 * (g % 4) + 32, 128 * dmc:128 * (dmc + 1)],
                        maskT[32 * (g % 4):32 * (g % 4) + 32,
                              128 * (g // 4) + 64 * hh:128 * (g // 4) + 64 * hh + 64],
                        start=True, stop=True, tile_position=(32 * (g % 4), 0)),
                        deps=([gev.get(pb)] + perm_guard) if dmc == 0 else [])
                gev[pb] = s.op("scalar", lambda e, gg=gg, pb=pb: e.copy(
                    perm[:].rearrange("p (d n g_) -> p d n g_", n=64, g_=NG2)[:, :, :, gg],
                    pg_ps[pb][:].rearrange("p (d n) -> p d n", n=64)), deps=[mm])
            perm_guard = []
            gather_done = list(gev.values())
            if DEBUG_L2 and gb == 0 and hh == 0:
                gather_done.append(s.op("sync", lambda e: e.dma_start(out=o_perm, in_=perm[:]),
                                        deps=gather_done, dma="dbg"))
            relu_ops = []
            eoev = {}
            w2pre = {}

            def pre_w2(j):
                jg = 16 * hh + j
                st2, bf2 = w2st[j % 2], w2b[j % 4]
                dw2 = s.op("sync", lambda e, jg=jg, st=st2: e.dma_start(
                    out=st[:],
                    in_=w2T_d[4 * jg:4 * jg + 4].rearrange("e f m -> (e f) m")),
                    deps=[w2_last.get(j % 4), w2c_last.get(j % 2)], dma="w2s%d" % (j % 2))
                cw2 = s.op("vector", lambda e, st=st2, bf=bf2: e.tensor_copy(bf[:], st[:]), deps=[dw2])
                w2c_last[j % 2] = cw2
                w2pre[j] = (cw2, bf2)

            def emit_w2(j):
                cw2, bf2 = w2pre.pop(j)
                for i in range(4):
                    esp = 4 * j + i
                    pb = esp % 2
                    mm = None
                    for dmc in range(8):
                        mm = s.op("tensor", lambda e, i=i, dmc=dmc, bf=bf2, j=j, pb=pb: e.matmul(
                            eo_ps[pb][:, NG2 * dmc:NG2 * (dmc + 1)],
                            bf[32 * i:32 * i + 32, 128 * dmc:128 * (dmc + 1)],
                            inner[32 * i:32 * i + 32, NG2 * j:NG2 * (j + 1)],
                            start=True, stop=True, tile_position=(32 * i, 0)),
                            deps=([cw2, relu_ops[j], eoev.get(pb)] + eo_guard) if dmc == 0 else [])
                        w2_last[j % 4] = mm
                    eoev[pb] = s.op("vector", lambda e, esp=esp, pb=pb, hh=hh: e.tensor_copy(
                        eo[hh][:].rearrange("p (d g_ n) -> p d g_ n", g_=NG2, n=64)[:, :, :, esp],
                        eo_ps[pb][:].rearrange("p (d g_) -> p d g_", g_=NG2)),
                        deps=[mm])
                eo_guard.clear()

            for j in range(16):
                jg = 16 * hh + j
                pre_w2(j)
                mm_stack = []
                for i in range(4):
                    esp = 4 * j + i
                    es = 64 * hh + esp
                    st, bf = w1st[esp % 4], w1b[esp % 4]
                    dw = s.op("sync", lambda e, es=es, st=st: e.dma_start(
                        out=st[:].rearrange("p (d f) -> p d f", f=32),
                        in_=w1_d[es:es + 1].rearrange("o (d p) f -> p (o d) f", p=128)),
                        deps=[w1_last.get(esp % 4)], dma="w1s%d" % (esp % 4))
                    cw_ = s.op("vector", lambda e, st=st, bf=bf: e.tensor_copy(bf[:], st[:]), deps=[dw])
                    for dmc in range(8):
                        mm = s.op("tensor", lambda e, esp=esp, i=i, dmc=dmc, bf=bf, j=j: e.matmul(
                            in_view(j)[32 * i:32 * i + 32, 0:NG2],
                            bf[:, 32 * dmc:32 * (dmc + 1)],
                            perm[:, 2048 * dmc + 32 * esp:2048 * dmc + 32 * esp + NG2],
                            start=(dmc == 0), stop=(dmc == 7), tile_position=(0, 32 * i)),
                            deps=[cw_] + (gather_done if esp == 0 else [])
                                 + ([relu_ops[j - 2]] if (dmc == 0 and i == 0 and j >= 2) else []))
                        mm_stack.append(mm)
                        w1_last[esp % 4] = mm
                relu_ops.append(s.op("scalar", lambda e, j=j, jg=jg: e.activation(
                    inner[:, NG2 * j:NG2 * (j + 1)], in_view(j)[:, 0:NG2], AF.Relu, bias=b1pk[:, jg:jg + 1]),
                    deps=mm_stack + [db1]))
                if j >= 1:
                    emit_w2(j - 1)
            emit_w2(15)
            eo_done += list(eoev.values())
            perm_guard += [w1_last[k] for k in w1_last] + [relu_ops[-1]]
        for gg in range(NG2):
            g = gb * NG2 + gg
            ib = gg % 2
            trs = []
            for hh in range(2):
                for dmc in range(8):
                    trs.append(s.op("tensor", lambda e, gg=gg, hh=hh, dmc=dmc: e.transpose(
                        eot_ps[64 * hh:64 * hh + 64, 128 * dmc:128 * (dmc + 1)],
                        eo[hh][:, 2048 * dmc + 64 * gg:2048 * dmc + 64 * gg + 64], ident[:],
                        tile_position=(0, 64 * hh)),
                        deps=(eo_done + [eot_guard]) if (hh == 0 and dmc == 0 and gg < 2) else
                             ([eot_guard] if (hh == 0 and dmc == 0) else [])))
            tev = s.op("scalar", lambda e, ib=ib: e.copy(eot[ib][:], eot_ps[:]), deps=[trs[-1]])
            eot_guard = tev
            mm = None
            first = (g % 4 == 0)
            for nh in range(2):
                mm = s.op("tensor", lambda e, g=g, nh=nh, ib=ib: e.matmul(
                    res_ps[32 * (g % 4):32 * (g % 4) + 32, 512 * nh:512 * nh + 512],
                    mask_all[:, GRP * g:GRP * (g + 1)],
                    eot[ib][:, 512 * nh:512 * nh + 512],
                    start=True, stop=False, tile_position=(0, 32 * (g % 4))),
                    deps=[tev] + ([res_evac] if (first and nh == 0) else []))
                mm = s.op("tensor", lambda e, g=g, nh=nh: e.matmul(
                    res_ps[32 * (g % 4):32 * (g % 4) + 32, 512 * nh:512 * nh + 512],
                    mask_all[:, GRP * g:GRP * (g + 1)],
                    b2b[:, 512 * nh:512 * nh + 512],
                    start=False, stop=True, tile_position=(0, 32 * (g % 4))),
                    deps=[cb2] if (g < 2 and nh == 0) else [])
            if g % 4 == 3:
                sg = g // 4
                rev = s.op("vector", lambda e, sg=sg: e.tensor_copy(ffn_sb[sg % 2][:], res_ps[:]),
                           deps=[mm, ffn_dma.get(sg % 2)])
                res_evac = rev
                ffn_dma[sg % 2] = s.op("sync", lambda e, sg=sg: e.dma_start(
                    out=o_ffn[128 * sg:128 * (sg + 1), :], in_=ffn_sb[sg % 2][:]), deps=[rev], dma="offn")
        eo_guard = [trs[-1]]
    C.close()
    stk = s.build()
    return nc, stk



def build_L3():
    nc = bass.Bass("TRN2", target_bir_lowering=False)
    x_own = nc.dram_tensor("x_own", [LH, DM], F32, kind="ExternalInput").ap()
    attn_a = nc.dram_tensor("attn_a", [LH, DM], BF16, kind="ExternalInput").ap()
    attn_b = nc.dram_tensor("attn_b", [LH, DM], BF16, kind="ExternalInput").ap()
    ffn = nc.dram_tensor("ffn", [LH, DM], F32, kind="ExternalInput").ap()
    o = nc.dram_tensor("o", [LH, DM], F32, kind="ExternalOutput").ap()

    s = Sched(nc)
    sb = nc.alloc_sbuf_tensor
    ta = [sb(f"ta{i}", [128, DM], F32) for i in range(2)]
    tb = [sb(f"tb{i}", [128, DM], BF16) for i in range(2)]
    tc_ = [sb(f"tc{i}", [128, DM], BF16) for i in range(2)]
    td = [sb(f"td{i}", [128, DM], F32) for i in range(2)]
    guard = {}
    for c in range(LH // 128):
        i = c % 2
        d1 = s.op("sync", lambda e, i=i, c=c: e.dma_start(
            out=ta[i][:], in_=x_own[128 * c:128 * (c + 1), :]), deps=guard.get(i, []), dma="la")
        d2 = s.op("sync", lambda e, i=i, c=c: e.dma_start(
            out=tb[i][:], in_=attn_a[128 * c:128 * (c + 1), :]), deps=guard.get(i, []), dma="lb")
        d3 = s.op("sync", lambda e, i=i, c=c: e.dma_start(
            out=tc_[i][:], in_=attn_b[128 * c:128 * (c + 1), :]), deps=guard.get(i, []), dma="lc")
        d4 = s.op("sync", lambda e, i=i, c=c: e.dma_start(
            out=td[i][:], in_=ffn[128 * c:128 * (c + 1), :]), deps=guard.get(i, []), dma="ld")
        a1 = s.op("vector", lambda e, i=i: e.tensor_add(ta[i][:], ta[i][:], tb[i][:]), deps=[d1, d2])
        a2 = s.op("vector", lambda e, i=i: e.tensor_add(ta[i][:], ta[i][:], tc_[i][:]), deps=[d3])
        a3 = s.op("vector", lambda e, i=i: e.tensor_add(ta[i][:], ta[i][:], td[i][:]), deps=[d4])
        dd = s.op("sync", lambda e, i=i, c=c: e.dma_start(
            out=o[128 * c:128 * (c + 1), :], in_=ta[i][:]), deps=[a3], dma="out")
        guard[i] = [dd, a3]
    stk = s.build()
    return nc, stk



_CACHE = {}


def _bfview(a):
    return a if a.dtype == ml_dtypes.bfloat16 else a.view(ml_dtypes.bfloat16)


def _get(name, builder):
    if name not in _CACHE:
        _CACHE[name] = builder()
    return _CACHE[name]


def kernel(L0_attn_hidden_states, L0_attn_in_proj_w, L0_attn_conv_w, L0_attn_conv_b,
           L0_attn_x_proj_w, L0_attn_dt_proj_w, L0_attn_A_log, L0_attn_D, L0_attn_out_proj_w,
           L0_ffn_Wc, L0_ffn_bc, L0_ffn_W1, L0_ffn_b1, L0_ffn_W2, L0_ffn_b2):
    f32 = np.float32
    bf16 = ml_dtypes.bfloat16
    cont = np.ascontiguousarray
    x = np.asarray(L0_attn_hidden_states, f32)
    W_in = np.asarray(L0_attn_in_proj_w, f32)
    conv_w = np.asarray(L0_attn_conv_w, f32)
    conv_b = np.asarray(L0_attn_conv_b, f32)
    xp_w = np.asarray(L0_attn_x_proj_w, f32)
    dtp_w = np.asarray(L0_attn_dt_proj_w, f32)
    A_log = np.asarray(L0_attn_A_log, f32)
    Dv = np.asarray(L0_attn_D, f32)
    W_out = np.asarray(L0_attn_out_proj_w, f32)
    Wc = np.asarray(L0_ffn_Wc, f32).reshape(DM, NES)
    bcv = np.asarray(L0_ffn_bc, f32).reshape(NES, 1)
    W1 = np.asarray(L0_ffn_W1, f32).reshape(NES, DM, 32)
    b1 = np.asarray(L0_ffn_b1, f32).reshape(NES, 32)
    W2 = np.asarray(L0_ffn_W2, f32).reshape(NES, DM, 32)
    b2 = np.asarray(L0_ffn_b2, f32).reshape(NES, DM)

    identb = np.eye(128, dtype=bf16)
    tie = np.linspace(0.0, 1e-6, GRP, dtype=f32)
    tie_full = cont(np.broadcast_to(np.tile(tie, NGRP)[None, :], (128, LH)))
    b1p = cont(b1.reshape(32, 128).T)
    w2T = cont(W2.transpose(0, 2, 1))
    W_inT_f = cont(W_in.T)
    xpT_f = cont(xp_w.T)
    dtpT_f = cont(dtp_w.T)
    WoT_f = cont(W_out.T)

    in1, in2_base, in3 = [], [], []
    for c in range(8):
        b, h = c // 2, c % 2
        sl = slice(DH * h, DH * h + DH)
        w_inT_c = cont(np.concatenate([W_inT_f[:, DH * h:DH * h + DH],
                                       W_inT_f[:, 2048 + DH * h:2048 + DH * h + DH]], axis=1))
        in1.append(dict(
            x_tm=cont(x[b]), w_inT=w_inT_c,
            convw=cont(conv_w[sl]), convb=cont(conv_b[sl].reshape(8, 128).T),
            xp_T=cont(xpT_f[sl]), identb=identb))
        in2_base.append(dict(
            dtpT=cont(dtpT_f[:, sl]), alog=cont(A_log[sl]),
            dvec=cont(Dv[sl].reshape(8, 128).T), woutT=cont(WoT_f[sl]),
            x_own=cont(x[b, LH * h:LH * h + LH]),
            wc=Wc, bc=bcv, tie=tie_full, w1=W1, b1p=b1p, w2T=w2T, b2=b2,
            identb=identb))

    nc1, _ = _get("L1", build_L1)
    r1 = run_bass_kernel_spmd(nc1, in1, core_ids=list(range(8)))

    in2 = []
    for c in range(8):
        pc = c ^ 1
        d = dict(in2_base[c])
        d["xbf"] = np.asarray(r1.results[c]["o_xbf"]).view(bf16) \
            if np.asarray(r1.results[c]["o_xbf"]).dtype != bf16 else np.asarray(r1.results[c]["o_xbf"])
        d["szbf"] = np.asarray(r1.results[c]["o_szbf"]).view(bf16) \
            if np.asarray(r1.results[c]["o_szbf"]).dtype != bf16 else np.asarray(r1.results[c]["o_szbf"])
        d["xdbl0"] = np.asarray(r1.results[c]["o_xdbl"])
        d["xdbl1"] = np.asarray(r1.results[pc]["o_xdbl"])
        in2.append(d)

    nc2, _ = _get("L2", build_L2)
    r2 = run_bass_kernel_spmd(nc2, in2, core_ids=list(range(8)))

    in3 = []
    for c in range(8):
        b, h = c // 2, c % 2
        pc = c ^ 1
        in3.append(dict(
            x_own=cont(x[b, LH * h:LH * h + LH]),
            attn_a=cont(_bfview(np.asarray(r2.results[c]["o_attn"]))[LH * h:LH * h + LH]),
            attn_b=cont(_bfview(np.asarray(r2.results[pc]["o_attn"]))[LH * h:LH * h + LH]),
            ffn=np.asarray(r2.results[c]["o_ffn"])))
    nc3, _ = _get("L3", build_L3)
    r3 = run_bass_kernel_spmd(nc3, in3, core_ids=list(range(8)))

    out = np.empty((4, L, DM), f32)
    for c in range(8):
        b, h = c // 2, c % 2
        out[b, LH * h:LH * h + LH] = np.asarray(r3.results[c]["o"])
    return out
